# version 05
# baseline (speedup 1.0000x reference)
import numpy as np
import ml_dtypes
from contextlib import ExitStack
import concourse.bass as bass
import concourse.mybir as mybir
from concourse.alu_op_type import AluOpType as ALU
from concourse.bass_utils import run_bass_kernel_spmd

F32 = mybir.dt.float32
BF16 = mybir.dt.bfloat16
AF = mybir.ActivationFunctionType
AX = mybir.AxisListType

D = 1024
DC = 8
NP = 8
MKV = 4096
TOWN = 2048
QT = 512
NQT = TOWN // QT
NCH = MKV // 128
HID = 2048
HC = 16
NE = 8
TP = 1024
NTP = TOWN // TP
SCALE = 0.03125
EPS = 1e-5

DBG_NSUPER = 8
DBG_NTILE = NQT
DBG_NPAIR = NP
DBG_1A = 9


class Sem:
    def __init__(self, nc, name):
        self.h = nc.alloc_semaphore(name)
        self.v = 0

    def inc(self, inst, n=1):
        inst.then_inc(self.h, n)
        self.v += n
        return self.v

    def dma(self, inst):
        inst.then_inc(self.h, 16)
        self.v += 16
        return self.v


def build_nc(y_external=False, phases=(1, 2), y_input=False, dbg2=False):
    nc = bass.Bass("TRN2", target_bir_lowering=False)
    E = {}

    for nm, sh in [("x_kv", [MKV, D]), ("x_own", [TOWN, D]),
                   ("wqT", [D, D]), ("wkT", [D, D]), ("wvT", [D, D]), ("woT", [D, D]),
                   ("bq", [128, NP]), ("bk", [128, NP]), ("bv", [1, D]), ("bo", [128, DC]),
                   ("Wr", [D, NE]), ("rbt", [1, 64]),
                   ("W1", [NE, D, HID]), ("W2", [NE, HID, D]), ("W3", [NE, D, HID]),
                   ("Ws1", [D, HID]), ("Ws2", [HID, D]), ("Ws3", [D, HID]),
                   ("identf", [128, 128])]:
        E[nm] = nc.dram_tensor(nm, sh, F32, kind="ExternalInput")
    E["identb"] = nc.dram_tensor("identb", [128, 128], BF16, kind="ExternalInput")
    E["sele"] = nc.dram_tensor("sele", [8, NE * 128], BF16, kind="ExternalInput")
    E["hmask"] = nc.dram_tensor("hmask", [128, 2], F32, kind="ExternalInput")
    E["out_d"] = nc.dram_tensor("out", [TOWN, D], F32, kind="ExternalOutput")
    E["y_scr"] = nc.dram_tensor("y_scr", [TOWN, D], F32,
                                kind="ExternalInput" if y_input else
                                ("ExternalOutput" if y_external else "Internal"))
    E["dbg2"] = dbg2
    if dbg2:
        E["dXL"] = nc.dram_tensor("dXL", [128, DC * 1024], BF16, kind="ExternalOutput")
        E["dWF"] = nc.dram_tensor("dWF", [128, 64], F32, kind="ExternalOutput")
        E["dOA"] = nc.dram_tensor("dOA", [128, DC * 1024], F32, kind="ExternalOutput")
        E["dMV"] = nc.dram_tensor("dMV", [128, 16], F32, kind="ExternalOutput")

    E["identb_s"] = nc.alloc_sbuf_tensor("identb_s", [128, 128], BF16)
    E["identf_s"] = nc.alloc_sbuf_tensor("identf_s", [128, 128], F32)
    E["ones32"] = nc.alloc_sbuf_tensor("ones32", [1, 128], F32)
    E["bq_s"] = nc.alloc_sbuf_tensor("bq_s", [128, NP], F32)
    E["bk_s"] = nc.alloc_sbuf_tensor("bk_s", [128, NP], F32)
    E["bo_s"] = nc.alloc_sbuf_tensor("bo_s", [128, DC], F32)
    E["bvrow"] = nc.alloc_sbuf_tensor("bvrow", [1, D], BF16)
    E["KA8"] = nc.alloc_sbuf_tensor("KA8", [128, NP], F32)
    E["hmask_s"] = nc.alloc_sbuf_tensor("hmask_s", [128, 2], F32)
    E["ST6"] = nc.alloc_sbuf_tensor("ST6", [128, 8, 12], F32)
    E["MV"] = nc.alloc_sbuf_tensor("MV", [128, 8, 2], F32)
    E["LNV"] = nc.alloc_sbuf_tensor("LNV", [128, 8], F32)
    E["EPSV"] = nc.alloc_sbuf_tensor("EPSV", [128, 1], F32)
    E["onesb16"] = nc.alloc_sbuf_tensor("onesb16", [1, 128], BF16)
    E["RSTD"] = nc.alloc_sbuf_tensor("RSTD", [128, 8], F32)

    for s in ("sX", "sSt", "sG", "sT", "sV", "sA", "sWF"):
        E[s] = Sem(nc, s)
    E["sW1"] = [Sem(nc, f"sW1_{i}") for i in range(4)]
    E["sW3"] = [Sem(nc, f"sW3_{i}") for i in range(4)]
    E["sW2"] = [Sem(nc, f"sW2_{i}") for i in range(2)]
    sX, sSt, sG, sT, sV, sA = (E[k] for k in ("sX", "sSt", "sG", "sT", "sV", "sA"))
    TE, VE, AC, GP, SY = nc.tensor, nc.vector, nc.scalar, nc.gpsimd, nc.sync
    E["GPQ"] = GP

    def ln_stats(xt_ap, n_tsub):
        ST6, MV, LNV, RSTD = E["ST6"], E["MV"], E["LNV"], E["RSTD"]
        for j in range(n_tsub):
            VE.bn_stats(ST6[:, j, 0:6], xt_ap[:, j, 0:512])
            VE.bn_stats(ST6[:, j, 6:12], xt_ap[:, j, 512:1024])
        VE.drain()
        for j in range(n_tsub):
            VE.bn_aggr(MV[:, j, :], ST6[:, j, :])
        sV.inc(VE.drain())
        AC.wait_ge(sV.h, sV.v)
        AC.activation(LNV[:, 0:n_tsub], MV[:, 0:n_tsub, 1], AF.Ln,
                      bias=E["EPSV"][:], scale=1.0)
        AC.drain()
        inst = AC.activation(RSTD[:, 0:n_tsub], LNV[:, 0:n_tsub], AF.Exp,
                             bias=0.0, scale=-0.5)
        sA.inc(inst)

    E["ln_stats"] = ln_stats

    with nc.Block():
        for dst, src in [(E["identb_s"], E["identb"]), (E["identf_s"], E["identf"]),
                         (E["bq_s"], E["bq"]), (E["bk_s"], E["bk"]),
                         (E["bo_s"], E["bo"]), (E["hmask_s"], E["hmask"])]:
            sX.dma(SY.dma_start(out=dst[:], in_=src[:]))
        sG.dma(GP.dma_start(out=E["bvrow"][:], in_=E["bv"][:]))
        VE.wait_ge(sX.h, sX.v)
        VE.memset(E["ones32"][:], 1.0)
        VE.memset(E["onesb16"][:], 1.0)
        sV.inc(VE.memset(E["EPSV"][:], EPS))

        if 1 in phases:
            emit_phase1(nc, E)
        nc.all_engine_barrier()
        if 2 in phases:
            emit_phase2(nc, E)
        AC.wait_ge(sSt.h, sSt.v)
    return nc


def emit_phase1(nc, E):
    TE, VE, AC, GP, SY = nc.tensor, nc.vector, nc.scalar, nc.gpsimd, nc.sync
    sX, sSt, sG, sT, sV, sA = (E[k] for k in ("sX", "sSt", "sG", "sT", "sV", "sA"))
    x_kv, x_own, y_scr = E["x_kv"], E["x_own"], E["y_scr"]
    bq, bk, bo = E["bq_s"], E["bk_s"], E["bo_s"]
    identb, identf, ones32 = E["identb_s"], E["identf_s"], E["ones32"]
    KA8 = E["KA8"]
    MV, RSTD = E["MV"], E["RSTD"]
    ln_stats = E["ln_stats"]

    with ExitStack() as _esp1:
        _e0 = _esp1.enter_context
        KF = _e0(nc.sbuf_tensor("KF", [128, NP, MKV], BF16))
        VA = _e0(nc.sbuf_tensor("VA", [128, NCH, 1040], BF16))
        XT = _e0(nc.sbuf_tensor("XT", [128, 4, 1024], F32))
        XLNB = _e0(nc.sbuf_tensor("XLNB", [128, 1024], BF16))
        with ExitStack() as _es1a:
            _e = _es1a.enter_context
            WA = _e(nc.sbuf_tensor("WA", [128, DC, 1024], BF16))
            WB = _e(nc.sbuf_tensor("WB", [128, DC, 1024], BF16))
            XLNF = _e(nc.sbuf_tensor("XLNF", [128, DC, 512], BF16))
            bvr = _e(nc.sbuf_tensor("bvr", [128, D], F32))
            psTb = _e(nc.psum_tensor("psTb", [128, DC, 128], BF16))
            psK = _e(nc.psum_tensor("psK", [128, 2, 512], F32))
            psV = _e(nc.psum_tensor("psV", [128, 2, 512], F32))
            psM = _e(nc.psum_tensor("psM", [128, 512], F32))
            sG.dma(GP.dma_start(out=WA[:],
                                in_=E["wkT"].rearrange("(c p) e -> p c e", p=128)))
            w_arr = sG.dma(GP.dma_start(
                out=WB[:], in_=E["wvT"].rearrange("(c p) e -> p c e", p=128)))
            VE.wait_ge(sX.h, sX.v)
            if DBG_1A < 1:
                return
            for h in range(16):
                inst = VE.memset(VA[:, :, 65 * h + 64: 65 * h + 65], 1.0)
            sV.inc(inst)
            TE.wait_ge(sV.h, sV.v)
            TE.wait_ge(sG.h, sG.v)
            for half in range(2):
                if half > 0:
                    TE.wait_ge(sV.h, sV.v)
                sT.inc(TE.matmul(psM[:], E["onesb16"][:],
                                 E["bvrow"][:, half * 512:(half + 1) * 512],
                                 start=True, stop=True))
                VE.wait_ge(sT.h, sT.v)
                sV.inc(VE.tensor_copy(bvr[:, half * 512:(half + 1) * 512], psM[:]))

            xfree = sV.v
            for s in range(DBG_NSUPER):
                SY.wait_ge(sV.h, xfree)
                sX.dma(SY.dma_start(
                    out=XT[:],
                    in_=x_kv.rearrange("(s t p) d -> s p t d", t=4, p=128)[s]))
                VE.wait_ge(sX.h, sX.v)
                if DBG_1A < 2:
                    continue
                ln_stats(XT, 4)
                VE.wait_ge(sA.h, sA.v)
                for j in range(4):
                    inst = VE.tensor_scalar(XLNB[:], XT[:, j, :], MV[:, j, 0:1],
                                            RSTD[:, j:j + 1], ALU.subtract, ALU.mult)
                    sV.inc(inst)
                    if j == 3:
                        xfree = sV.v
                    if DBG_1A < 3:
                        continue
                    TE.wait_ge(sV.h, sV.v)
                    for dc in range(DC):
                        inst = TE.transpose(psTb[:, dc, :],
                                            XLNB[:, dc * 128:(dc + 1) * 128], identb[:])
                    sT.inc(inst)
                    VE.wait_ge(sT.h, sT.v)
                    inst = VE.tensor_copy(XLNF[:, :, j * 128:(j + 1) * 128], psTb[:])
                    sV.inc(inst)
                if DBG_1A < 4:
                    continue
                TE.wait_ge(sV.h, sV.v)
                if s == 0:
                    TE.wait_ge(sG.h, w_arr)
                evK = sV.v
                for p in range(NP):
                    if p >= 2:
                        TE.wait_ge(sV.h, evK + p - 1)
                    for dc in range(DC):
                        inst = TE.matmul(psK[:, p % 2, :],
                                         WA[:, dc, 128 * p:128 * (p + 1)],
                                         XLNF[:, dc, :],
                                         start=(dc == 0), stop=(dc == DC - 1))
                    sT.inc(inst)
                    VE.wait_ge(sT.h, sT.v)
                    inst = VE.tensor_scalar(KF[:, p, 512 * s:512 * (s + 1)],
                                            psK[:, p % 2, :], bk[:, p:p + 1],
                                            None, ALU.add)
                    sV.inc(inst)
                evV = sV.v
                if DBG_1A < 5:
                    continue
                for ms in range(4):
                    ch = 4 * s + ms
                    for half in range(2):
                        idx = ms * 2 + half
                        if idx >= 2:
                            TE.wait_ge(sV.h, evV + idx - 1)
                        for dc in range(DC):
                            inst = TE.matmul(psV[:, idx % 2, :],
                                             XLNF[:, dc, ms * 128:(ms + 1) * 128],
                                             WB[:, dc, half * 512:(half + 1) * 512],
                                             start=(dc == 0), stop=(dc == DC - 1))
                        sT.inc(inst)
                        VE.wait_ge(sT.h, sT.v)
                        inst = VE.tensor_tensor(
                            out=VA[:, ch, 520 * half:520 * half + 520]
                                .rearrange("p (h cc) -> p h cc", cc=65)[:, :, 0:64],
                            in0=psV[:, idx % 2, :].rearrange("p (h cc) -> p h cc", cc=64),
                            in1=bvr[:, half * 512:(half + 1) * 512]
                                .rearrange("p (h cc) -> p h cc", cc=64),
                            op=ALU.add)
                        sV.inc(inst)
            if DBG_1A < 6:
                return
            pass

        nc.all_engine_barrier()

        with ExitStack() as _es1b:
            _e = _es1b.enter_context
            Wsh = _e(nc.sbuf_tensor("Wsh", [128, DC, 1024], BF16))
            XLNF2 = _e(nc.sbuf_tensor("XLNF2", [128, DC, 512], BF16))
            QFm = _e(nc.sbuf_tensor("QFm", [128, NP, 512], BF16))
            PT = _e(nc.sbuf_tensor("PT", [128, 2, 2, 1024], BF16))
            HDS = _e(nc.sbuf_tensor("HDS", [128, NP, 512], BF16))
            tmpB = _e(nc.sbuf_tensor("tmpB", [64, 512], BF16))
            LROW = _e(nc.sbuf_tensor("LROW", [1, 2, 512], BF16))
            LINV = _e(nc.sbuf_tensor("LINV", [64, 512], BF16))
            PRJ = _e(nc.sbuf_tensor("PRJ", [128, 128], F32))
            YTM = _e(nc.sbuf_tensor("YTM", [128, 1024], F32))
            CMS = _e(nc.sbuf_tensor("CMS", [128, 16], F32))
            CM1 = _e(nc.sbuf_tensor("CM1", [128, 1], F32))
            Gt = _e(nc.sbuf_tensor("Gt", [1, 4], F32))
            NEGG = _e(nc.sbuf_tensor("NEGG", [128, 1], F32))
            psS_A = _e(nc.psum_tensor("psS_A", [128, 2, 512], F32))
            psS_B = _e(nc.psum_tensor("psS_B", [128, 2, 512], F32))
            psPV_A = _e(nc.psum_tensor("psPV_A", [128, 512], F32))
            psPV_B = _e(nc.psum_tensor("psPV_B", [128, 512], F32))
            psM1 = _e(nc.psum_tensor("psM1", [128, 512], F32))
            psTb2 = _e(nc.psum_tensor("psTb2", [128, DC, 128], BF16))
            woDone = sT.v
            yDone = sSt.v
            for t in range(DBG_NTILE):
                if t > 0:
                    GP.wait_ge(sT.h, woDone)
                wq_arr = sG.dma(GP.dma_start(
                    out=Wsh[:], in_=E["wqT"].rearrange("(c p) e -> p c e", p=128)))
                if t > 0:
                    SY.wait_ge(sSt.h, yDone)
                sX.dma(SY.dma_start(
                    out=XT[:],
                    in_=x_own.rearrange("(s t p) d -> s p t d", t=4, p=128)[t]))
                VE.wait_ge(sX.h, sX.v)
                ln_stats(XT, 4)
                VE.wait_ge(sA.h, sA.v)
                for j in range(4):
                    inst = VE.tensor_scalar(XLNB[:], XT[:, j, :], MV[:, j, 0:1],
                                            RSTD[:, j:j + 1], ALU.subtract, ALU.mult)
                    sV.inc(inst)
                    TE.wait_ge(sV.h, sV.v)
                    for dc in range(DC):
                        inst = TE.transpose(psTb2[:, dc, :],
                                            XLNB[:, dc * 128:(dc + 1) * 128], identb[:])
                    sT.inc(inst)
                    VE.wait_ge(sT.h, sT.v)
                    inst = VE.tensor_copy(XLNF2[:, :, j * 128:(j + 1) * 128], psTb2[:])
                    sV.inc(inst)
                TE.wait_ge(sV.h, sV.v)
                TE.wait_ge(sG.h, wq_arr)
                evQ = sV.v
                for p in range(NP):
                    if p >= 1:
                        TE.wait_ge(sV.h, evQ + p)
                    for dc in range(DC):
                        inst = TE.matmul(psM1[:], Wsh[:, dc, 128 * p:128 * (p + 1)],
                                         XLNF2[:, dc, :],
                                         start=(dc == 0), stop=(dc == DC - 1))
                    sT.inc(inst)
                    qDone = sT.v
                    VE.wait_ge(sT.h, sT.v)
                    inst = VE.tensor_scalar(QFm[:, p, :], psM1[:], bq[:, p:p + 1],
                                            None, ALU.add)
                    sV.inc(inst)
                GP.wait_ge(sT.h, qDone)
                wo_arr = sG.dma(GP.dma_start(
                    out=Wsh[:], in_=E["woT"].rearrange("(c p) e -> p c e", p=128)))
                epiV = sV.v
                shiftSt = sSt.v
                for p in range(DBG_NPAIR):
                    TE.wait_ge(sV.h, sV.v)
                    smBase = sV.v
                    for idx, cj in enumerate(range(0, NCH, 4)):
                        if idx >= 2:
                            TE.wait_ge(sV.h, smBase + idx - 1)
                        TE.matmul(psS_A[:, idx % 2, :],
                                  KF[0:64, p, 128 * cj:128 * (cj + 1)],
                                  QFm[0:64, p, :], start=True, stop=True,
                                  tile_position=(0, 0))
                        inst = TE.matmul(psS_B[:, idx % 2, :],
                                         KF[64:128, p, 128 * cj:128 * (cj + 1)],
                                         QFm[64:128, p, :], start=True, stop=True,
                                         tile_position=(64, 0))
                        sT.inc(inst)
                        VE.wait_ge(sT.h, sT.v)
                        sampA = psS_A[:, idx % 2, :].rearrange(
                            "p (a b) -> p a b", b=4)[:, :, 0]
                        sampB = psS_B[:, idx % 2, :].rearrange(
                            "p (a b) -> p a b", b=4)[:, :, 0]
                        VE.reduce_max(CMS[:, 2 * idx:2 * idx + 1], sampA, axis=AX.X)
                        inst = VE.reduce_max(CMS[:, 2 * idx + 1:2 * idx + 2], sampB,
                                             axis=AX.X)
                        sV.inc(inst)
                    VE.drain()
                    VE.reduce_max(CM1[:], CMS[:], axis=AX.X)
                    sV.inc(VE.drain())
                    TE.wait_ge(sV.h, sV.v)
                    sT.inc(TE.transpose(psM1[0:1, 0:128], CM1[:], identf[:]))
                    VE.wait_ge(sT.h, sT.v)
                    VE.reduce_max(Gt[:, 0:1], psM1[0:1, 0:128], axis=AX.X)
                    VE.drain()
                    VE.tensor_scalar(Gt[:, 1:2], Gt[:, 0:1], SCALE, 20.0,
                                     ALU.mult, ALU.add)
                    VE.drain()
                    VE.tensor_scalar(Gt[:, 2:3], Gt[:, 1:2], -1.0, None, ALU.mult)
                    sV.inc(VE.drain())
                    TE.wait_ge(sV.h, sV.v)
                    sT.inc(TE.matmul(psM1[:, 0:1], ones32[:], Gt[:, 2:3],
                                     start=True, stop=True))
                    VE.wait_ge(sT.h, sT.v)
                    negg_v = sV.inc(VE.tensor_copy(NEGG[:], psM1[:, 0:1]))
                    AC.wait_ge(sV.h, negg_v)
                    TE.wait_ge(sV.h, negg_v)
                    tBase = sT.v
                    aBase = sA.v
                    for k in range(17):
                        if k < 16:
                            if k >= 1:
                                TE.wait_ge(sA.h, aBase + 2 * k)
                            for u in range(2):
                                j = 2 * k + u
                                sT.inc(TE.matmul(
                                    psS_A[:, u, :], KF[0:64, p, 128 * j:128 * (j + 1)],
                                    QFm[0:64, p, :], start=True, stop=True,
                                    tile_position=(0, 0)))
                            for u in range(2):
                                j = 2 * k + u
                                sT.inc(TE.matmul(
                                    psS_B[:, u, :], KF[64:128, p, 128 * j:128 * (j + 1)],
                                    QFm[64:128, p, :], start=True, stop=True,
                                    tile_position=(64, 0)))
                        if k >= 1:
                            kk = k - 1
                            TE.wait_ge(sA.h, aBase + 2 * kk + 2)
                            if p > 0 and kk == 0:
                                TE.wait_ge(sV.h, epiV)
                            for u in range(2):
                                j = 2 * kk + u
                                sT.inc(TE.matmul(
                                    psPV_A[0:65, :],
                                    VA[:, j, 65 * (2 * p):65 * (2 * p) + 65],
                                    PT[:, 0, kk % 2, u * 512:(u + 1) * 512],
                                    start=(j == 0), stop=(j == NCH - 1)))
                                sT.inc(TE.matmul(
                                    psPV_B[0:65, :],
                                    VA[:, j, 65 * (2 * p + 1):65 * (2 * p + 1) + 65],
                                    PT[:, 1, kk % 2, u * 512:(u + 1) * 512],
                                    start=(j == 0), stop=(j == NCH - 1)))
                        if k < 16:
                            AC.wait_ge(sT.h,
                                       tBase + (2 if k == 0 else 4 + 8 * (k - 1) + 2))
                            sA.inc(AC.activation(
                                PT[:, 0, k % 2, :],
                                psS_A.rearrange("p a b -> p (a b)"),
                                AF.Exp, bias=NEGG[:], scale=SCALE))
                            AC.wait_ge(sT.h,
                                       tBase + (4 if k == 0 else 4 + 8 * (k - 1) + 4))
                            sA.inc(AC.activation(
                                PT[:, 1, k % 2, :],
                                psS_B.rearrange("p a b -> p (a b)"),
                                AF.Exp, bias=NEGG[:], scale=SCALE))
                    VE.wait_ge(sT.h, sT.v)
                    with nc.allow_low_precision(
                            reason="1/l stored bf16; heads are bf16 anyway"):
                        VE.reciprocal(LROW[:, 0, :], psPV_A[64:65, :])
                        sV.inc(VE.reciprocal(LROW[:, 1, :], psPV_B[64:65, :]))
                    TE.wait_ge(sV.h, sV.v)
                    sT.inc(TE.matmul(psM1[0:64, :], E["onesb16"][0:1, 0:64],
                                     LROW[:, 0, :], start=True, stop=True))
                    VE.wait_ge(sT.h, sT.v)
                    sV.inc(VE.tensor_scalar(LINV[:], psM1[0:64, :], 0.0, None, ALU.add))
                    VE.drain()
                    VE.tensor_tensor(out=HDS[0:64, p, :], in0=psPV_A[0:64, :],
                                     in1=LINV[:], op=ALU.mult)
                    TE.wait_ge(sV.h, sV.v)
                    sT.inc(TE.matmul(psM1[0:64, :], E["onesb16"][0:1, 0:64],
                                     LROW[:, 1, :], start=True, stop=True))
                    VE.wait_ge(sT.h, sT.v)
                    VE.tensor_scalar(LINV[:], psM1[0:64, :], 0.0, None, ALU.add)
                    VE.drain()
                    if p > 0:
                        VE.wait_ge(sSt.h, shiftSt + 16 * p)
                    inst = VE.tensor_tensor(out=tmpB[:], in0=psPV_B[0:64, :],
                                            in1=LINV[:], op=ALU.mult)
                    epiV = sV.inc(inst)
                    AC.wait_ge(sV.h, epiV)
                    sSt.dma(AC.dma_start(out=HDS[64:128, p, :], in_=tmpB[:]))

                TE.wait_ge(sSt.h, sSt.v)
                TE.wait_ge(sG.h, wo_arr)
                ytmV = sV.v
                for j in range(4):
                    psT = psS_A if j % 2 == 0 else psS_B
                    psTbig = psT.rearrange("p a b -> p (a b)")
                    psTv = psTbig.rearrange("p (c d) -> p c d", d=128)
                    if j >= 2:
                        TE.wait_ge(sV.h, ytmV)
                    evW = sV.v
                    for dc in range(DC):
                        wo_ps = psM1[:, 0:128] if dc % 2 == 0 else psPV_A[:, 0:128]
                        if dc >= 2:
                            TE.wait_ge(sV.h, evW + dc - 1)
                        for rc in range(DC):
                            inst = TE.matmul(wo_ps,
                                             Wsh[:, rc, 128 * dc:128 * (dc + 1)],
                                             HDS[:, rc, j * 128:(j + 1) * 128],
                                             start=(rc == 0), stop=(rc == DC - 1))
                        sT.inc(inst)
                        VE.wait_ge(sT.h, sT.v)
                        inst = VE.tensor_scalar(PRJ[:], wo_ps,
                                                bo[:, dc:dc + 1], None, ALU.add)
                        sV.inc(inst)
                        TE.wait_ge(sV.h, sV.v)
                        sT.inc(TE.transpose(psTv[:, dc, :], PRJ[:], identf[:]))
                    VE.wait_ge(sT.h, sT.v)
                    if j > 0:
                        VE.wait_ge(sSt.h, sSt.v)
                    VE.tensor_scalar(XLNB[:], XT[:, j, :], MV[:, j, 0:1],
                                     RSTD[:, j:j + 1], ALU.subtract, ALU.mult)
                    VE.tensor_tensor(out=YTM[:], in0=psTbig, in1=XT[:, j, :],
                                     op=ALU.add)
                    inst = VE.tensor_tensor(out=YTM[:], in0=YTM[:], in1=XLNB[:],
                                            op=ALU.add)
                    ytmV = sV.inc(inst)
                    AC.wait_ge(sV.h, ytmV)
                    sSt.dma(AC.dma_start(
                        out=y_scr.rearrange("(r p) d -> r p d", p=128)[4 * t + j],
                        in_=YTM[:]))
                woDone = sT.v
                yDone = sSt.v
            SY.wait_ge(sSt.h, sSt.v)


def emit_phase2(nc, E):
    TE, VE, AC, GP, SY = nc.tensor, nc.vector, nc.scalar, nc.gpsimd, nc.sync
    sX, sSt, sG, sT, sV, sA, sWF = (E[k] for k in
                                    ("sX", "sSt", "sG", "sT", "sV", "sA", "sWF"))
    y_scr, out_d = E["y_scr"], E["out_d"]
    identf, ones32 = E["identf_s"], E["ones32"]
    MV, RSTD = E["MV"], E["RSTD"]
    ln_stats = E["ln_stats"]

    def w1pos(e, hc):
        return e * 40 + 2 * hc

    def w3pos(e, hc):
        return e * 40 + 2 * hc + 1

    def w2pos(e, dcc):
        return e * 40 + 32 + dcc

    NPOS = (NE + 1) * 40

    with ExitStack() as _es2:
        _e2 = _es2.enter_context
        YR = _e2(nc.sbuf_tensor("YR", [128, 8, 1024], F32))
        XL2 = _e2(nc.sbuf_tensor("XL2", [128, DC, 1024], BF16))
        XL32 = _e2(nc.sbuf_tensor("XL32", [128, 1024], F32))
        X32R = _e2(nc.sbuf_tensor("X32R", [128, 2, DC, 128], F32))
        HW = _e2(nc.sbuf_tensor("HW", [128, HC, 1024], BF16))
        OACC = _e2(nc.sbuf_tensor("OACC", [128, DC, 1024], F32))
        WS1 = _e2(nc.sbuf_tensor("WS1", [128, 4, DC, 128], BF16))
        WS3 = _e2(nc.sbuf_tensor("WS3", [128, 4, DC, 128], BF16))
        WS2 = _e2(nc.sbuf_tensor("WS2", [128, 2, HC, 128], BF16))
        WREP = _e2(nc.sbuf_tensor("WREP", [128, NE, 1024], BF16))
        SILB = _e2(nc.sbuf_tensor("SILB", [128, 2, 1024], BF16))
        HTMP = _e2(nc.sbuf_tensor("HTMP", [128, 2, 1024], BF16))
        Wr32 = _e2(nc.sbuf_tensor("Wr32", [128, DC, 8], F32))
        RBR = _e2(nc.sbuf_tensor("RBR", [128, 64], F32))
        LG = _e2(nc.sbuf_tensor("LG", [128, 8, 8], F32))
        A8 = _e2(nc.sbuf_tensor("A8", [128, 8, 8], F32))
        MK1 = _e2(nc.sbuf_tensor("MK1", [128, 8, 8], F32))
        MK2 = _e2(nc.sbuf_tensor("MK2", [128, 8, 8], F32))
        T8 = _e2(nc.sbuf_tensor("T8", [128, 8, 8], F32))
        WFL = _e2(nc.sbuf_tensor("WFL", [128, 8, 8], F32))
        M12 = _e2(nc.sbuf_tensor("M12", [128, 8, 2], F32))
        L12 = _e2(nc.sbuf_tensor("L12", [128, 8, 2], F32))
        E12 = _e2(nc.sbuf_tensor("E12", [128, 8, 2], F32))
        NEGM = _e2(nc.sbuf_tensor("NEGM", [128, 8], F32))
        ZZ = _e2(nc.sbuf_tensor("ZZ", [128, 8], F32))
        ZI = _e2(nc.sbuf_tensor("ZI", [128, 8], F32))
        WC = _e2(nc.sbuf_tensor("WC", [128, 8, 2], F32))
        WFT = _e2(nc.sbuf_tensor("WFT", [8, 1024], BF16))
        SELE = _e2(nc.sbuf_tensor("SELE", [8, NE, 128], BF16))
        OTM = _e2(nc.sbuf_tensor("OTM", [128, 2, 1024], F32))
        psA = _e2(nc.psum_tensor("psA", [128, 2, 1024], F32))
        psB = _e2(nc.psum_tensor("psB", [128, 2, 1024], F32))
        sX.dma(SY.dma_start(out=Wr32[:],
                            in_=E["Wr"].rearrange("(c p) e -> p c e", p=128)))
        sX.dma(SY.dma_start(out=RBR[0:1, :], in_=E["rbt"][:]))
        TE.wait_ge(sX.h, sX.v)
        sT.inc(TE.matmul(psA[:, 0, 0:64], ones32[:], RBR[0:1, :],
                         start=True, stop=True))
        VE.wait_ge(sT.h, sT.v)
        sV.inc(VE.tensor_copy(RBR[:, :], psA[:, 0, 0:64]))
        sX.dma(SY.dma_start(out=SELE.rearrange("p a b -> p (a b)"), in_=E["sele"][:]))

        for tp in range(NTP):
            wfBase = sWF.v
            arrival = {}
            for e in range(NE + 1):
                for hc in range(HC):
                    for (WT, src, pos, ring, sems, kind) in (
                        (WS1, E["W1"][e] if e < NE else E["Ws1"], w1pos(e, hc), 4,
                         E["sW1"], "w1"),
                        (WS3, E["W3"][e] if e < NE else E["Ws3"], w3pos(e, hc), 4,
                         E["sW3"], "w3"),
                    ):
                        prev_hc = hc - ring
                        if prev_hc >= 0:
                            GP.wait_ge(sWF.h, wfBase + pos - 2 * ring + 1)
                        elif e > 0:
                            prev_pos = (e - 1) * 40 + 2 * (HC + prev_hc) + (pos % 2)
                            GP.wait_ge(sWF.h, wfBase + prev_pos + 1)
                        elif tp > 0:
                            GP.wait_ge(sWF.h, wfBase)
                        sl = sems[hc % 4]
                        sl.dma(GP.dma_start(
                            out=WT[:, hc % 4, :, :],
                            in_=src.rearrange("(c p) h -> p c h", p=128)
                                  [:, :, hc * 128:(hc + 1) * 128]))
                        arrival[(kind, e, hc)] = (sl, sl.v)
                for dcc in range(DC):
                    pos = w2pos(e, dcc)
                    prev_dcc = dcc - 2
                    if prev_dcc >= 0:
                        GP.wait_ge(sWF.h, wfBase + pos - 2 + 1)
                    elif e > 0:
                        prev_pos = (e - 1) * 40 + 32 + (DC + prev_dcc)
                        GP.wait_ge(sWF.h, wfBase + prev_pos + 1)
                    elif tp > 0:
                        GP.wait_ge(sWF.h, wfBase)
                    sl = E["sW2"][dcc % 2]
                    sl.dma(GP.dma_start(
                        out=WS2[:, dcc % 2, :, :],
                        in_=(E["W2"][e] if e < NE else E["Ws2"])
                            .rearrange("(c p) d -> p c d", p=128)
                            [:, :, dcc * 128:(dcc + 1) * 128]))
                    arrival[("w2", e, dcc)] = (sl, sl.v)

            SY.wait_ge(sSt.h, sSt.v)
            SY.wait_ge(sV.h, sV.v)
            sX.dma(SY.dma_start(
                out=YR[:],
                in_=y_scr.rearrange("(q s p) d -> q p s d", s=8, p=128)[tp]))
            VE.wait_ge(sX.h, sX.v)
            ln_stats(YR, 8)
            VE.wait_ge(sA.h, sA.v)
            rEv = sV.v
            for j in range(8):
                par = j % 2
                inst = VE.tensor_scalar(XL32[:], YR[:, j, :], MV[:, j, 0:1],
                                        RSTD[:, j:j + 1], ALU.subtract, ALU.mult)
                sV.inc(inst)
                TE.wait_ge(sV.h, sV.v)
                if j >= 2:
                    TE.wait_ge(sV.h, rEv)
                psTT = psA[:, par, :].rearrange("p (c d) -> p c d", d=128)
                for dc in range(DC):
                    inst = TE.transpose(psTT[:, dc, :],
                                        XL32[:, dc * 128:(dc + 1) * 128], identf[:])
                sT.inc(inst)
                VE.wait_ge(sT.h, sT.v)
                VE.tensor_scalar(XL2[:, :, j * 128:(j + 1) * 128], psTT[:, :, :],
                                 0.0, None, ALU.add)
                inst = VE.tensor_copy(X32R[:, par, :, :], psTT[:, :, :])
                rEv = sV.inc(inst)
                TE.wait_ge(sV.h, rEv)
                for dc in range(DC):
                    inst = TE.matmul(psB[:, par, 0:8], X32R[:, par, dc, :],
                                     Wr32[:, dc, :],
                                     start=(dc == 0), stop=(dc == DC - 1))
                sT.inc(inst)
                VE.wait_ge(sT.h, sT.v)
                inst = VE.tensor_scalar(LG[:, j, :], psB[:, par, 0:8], SCALE,
                                        None, ALU.mult)
                sV.inc(inst)
            VE.drain()
            VE.tensor_tensor(out=A8.rearrange("p a b -> p (a b)"),
                             in0=LG.rearrange("p a b -> p (a b)"), in1=RBR[:],
                             op=ALU.add)
            VE.drain()
            for j in range(8):
                VE.reduce_max(M12[:, j, 0:1], A8[:, j, :], axis=AX.X)
            VE.drain()
            for j in range(8):
                VE.tensor_scalar(MK1[:, j, :], A8[:, j, :], M12[:, j, 0:1],
                                 None, ALU.is_ge)
            VE.drain()
            VE.tensor_tensor(out=T8.rearrange("p a b -> p (a b)"),
                             in0=LG.rearrange("p a b -> p (a b)"),
                             in1=MK1.rearrange("p a b -> p (a b)"), op=ALU.mult)
            VE.drain()
            for j in range(8):
                VE.reduce_sum(L12[:, j, 0:1], T8[:, j, :], axis=AX.X)
            VE.scalar_tensor_tensor(out=A8.rearrange("p a b -> p (a b)"),
                                    in0=MK1.rearrange("p a b -> p (a b)"),
                                    scalar=-1e30,
                                    in1=A8.rearrange("p a b -> p (a b)"),
                                    op0=ALU.mult, op1=ALU.add)
            VE.drain()
            for j in range(8):
                VE.reduce_max(M12[:, j, 1:2], A8[:, j, :], axis=AX.X)
            VE.drain()
            for j in range(8):
                VE.tensor_scalar(MK2[:, j, :], A8[:, j, :], M12[:, j, 1:2],
                                 None, ALU.is_ge)
            VE.drain()
            VE.tensor_tensor(out=T8.rearrange("p a b -> p (a b)"),
                             in0=LG.rearrange("p a b -> p (a b)"),
                             in1=MK2.rearrange("p a b -> p (a b)"), op=ALU.mult)
            VE.drain()
            for j in range(8):
                VE.reduce_sum(L12[:, j, 1:2], T8[:, j, :], axis=AX.X)
            VE.drain()
            VE.tensor_tensor(out=NEGM[:], in0=L12[:, :, 0], in1=L12[:, :, 1],
                             op=ALU.max)
            VE.drain()
            VE.tensor_scalar(NEGM[:], NEGM[:], -1.0, None, ALU.mult)
            sV.inc(VE.drain())
            AC.wait_ge(sV.h, sV.v)
            for j in range(8):
                AC.activation(E12[:, j, 0:1], L12[:, j, 0:1], AF.Exp,
                              bias=NEGM[:, j:j + 1], scale=1.0)
                inst = AC.activation(E12[:, j, 1:2], L12[:, j, 1:2], AF.Exp,
                                     bias=NEGM[:, j:j + 1], scale=1.0)
            sA.inc(inst)
            VE.wait_ge(sA.h, sA.v)
            VE.tensor_tensor(out=ZZ[:], in0=E12[:, :, 0], in1=E12[:, :, 1], op=ALU.add)
            VE.drain()
            VE.reciprocal(ZI[:], ZZ[:])
            VE.drain()
            VE.tensor_tensor(out=WC[:, :, 0], in0=E12[:, :, 0], in1=ZI[:], op=ALU.mult)
            VE.tensor_tensor(out=WC[:, :, 1], in0=E12[:, :, 1], in1=ZI[:], op=ALU.mult)
            VE.drain()
            for j in range(8):
                VE.tensor_scalar(T8[:, j, :], MK1[:, j, :], WC[:, j, 0:1],
                                 None, ALU.mult)
            VE.drain()
            for j in range(8):
                VE.scalar_tensor_tensor(out=WFL[:, j, :], in0=MK2[:, j, :],
                                        scalar=WC[:, j, 1:2], in1=T8[:, j, :],
                                        op0=ALU.mult, op1=ALU.add)
            sV.inc(VE.drain())
            TE.wait_ge(sV.h, sV.v)
            for j in range(8):
                inst = TE.transpose(psB[0:8, 0, j * 128:(j + 1) * 128],
                                    WFL[:, j, :], identf[:])
            sT.inc(inst)
            VE.wait_ge(sT.h, sT.v)
            sV.inc(VE.tensor_scalar(WFT[:], psB[0:8, 0, :], 0.0, None, ALU.add))
            TE.wait_ge(sV.h, sV.v)
            TE.wait_ge(sX.h, sX.v)
            repEv = sV.v
            for e in range(NE):
                for half in range(2):
                    if e > 0 or half > 0:
                        TE.wait_ge(sV.h, repEv)
                    sT.inc(TE.matmul(psA[:, 0, half * 512:(half + 1) * 512],
                                     SELE[:, e, :],
                                     WFT[:, half * 512:(half + 1) * 512],
                                     start=True, stop=True))
                    VE.wait_ge(sT.h, sT.v)
                    inst = VE.tensor_scalar(WREP[:, e, half * 512:(half + 1) * 512],
                                            psA[:, 0, half * 512:(half + 1) * 512],
                                            0.0, None, ALU.add)
                    repEv = sV.inc(inst)

            for e in range(NE + 1):
                eV = sV.v
                eA = sA.v
                hEv_base = sV.v
                silA_base = sA.v
                for hc in range(HC):
                    par = hc % 2
                    _s, _v = arrival[("w1", e, hc)]
                    TE.wait_ge(_s.h, _v)
                    if hc >= 2:
                        TE.wait_ge(sV.h, hEv_base + hc - 1)
                    else:
                        TE.wait_ge(sV.h, eV)
                    for u in range(2):
                        for dc in range(DC):
                            inst = TE.matmul(psA[:, par, u * 512:(u + 1) * 512],
                                             WS1[:, hc % 4, dc, :],
                                             XL2[:, dc, u * 512:(u + 1) * 512],
                                             start=(dc == 0), stop=(dc == DC - 1))
                    sWF.inc(inst)
                    aReady = sWF.v
                    _s, _v = arrival[("w3", e, hc)]
                    TE.wait_ge(_s.h, _v)
                    if hc >= 2:
                        TE.wait_ge(sA.h, silA_base + hc - 1)
                    elif e > 0 or tp > 0:
                        TE.wait_ge(sA.h, eA)
                    for u in range(2):
                        for dc in range(DC):
                            inst = TE.matmul(psB[:, par, u * 512:(u + 1) * 512],
                                             WS3[:, hc % 4, dc, :],
                                             XL2[:, dc, u * 512:(u + 1) * 512],
                                             start=(dc == 0), stop=(dc == DC - 1))
                    sWF.inc(inst)
                    AC.wait_ge(sWF.h, sWF.v)
                    inst = AC.activation(SILB[:, par, :], psB[:, par, :], AF.Silu,
                                         bias=0.0, scale=1.0)
                    sA.inc(inst)
                    VE.wait_ge(sWF.h, aReady)
                    VE.wait_ge(sA.h, sA.v)
                    VE.tensor_tensor(out=HTMP[:, par, :], in0=psA[:, par, :],
                                     in1=SILB[:, par, :], op=ALU.mult)
                    if e < NE:
                        inst = VE.tensor_tensor(out=HW[:, hc, :], in0=HTMP[:, par, :],
                                                in1=WREP[:, e, :], op=ALU.mult)
                    else:
                        inst = VE.tensor_copy(HW[:, hc, :], HTMP[:, par, :])
                    sV.inc(inst)
                cEv = sV.v
                for dcc in range(DC):
                    par = dcc % 2
                    _s, _v = arrival[("w2", e, dcc)]
                    TE.wait_ge(_s.h, _v)
                    TE.wait_ge(sV.h, hEv_base + HC)
                    if dcc >= 2:
                        TE.wait_ge(sV.h, cEv + dcc - 1)
                    for u in range(2):
                        for hcc in range(HC):
                            inst = TE.matmul(psA[:, par, u * 512:(u + 1) * 512],
                                             WS2[:, dcc % 2, hcc, :],
                                             HW[:, hcc, u * 512:(u + 1) * 512],
                                             start=(hcc == 0), stop=(hcc == HC - 1))
                    sWF.inc(inst)
                    VE.wait_ge(sWF.h, sWF.v)
                    if e == 0:
                        inst = VE.tensor_scalar(OACC[:, dcc, :], psA[:, par, :], 0.0,
                                                None, ALU.add)
                    else:
                        inst = VE.scalar_tensor_tensor(out=OACC[:, dcc, :],
                                                       in0=psA[:, par, :], scalar=0.0,
                                                       in1=OACC[:, dcc, :],
                                                       op0=ALU.add, op1=ALU.add)
                    sV.inc(inst)

            if E["dbg2"] and tp == NTP - 1:
                AC.wait_ge(sV.h, sV.v)
                sSt.dma(AC.dma_start(out=E["dXL"][:],
                                     in_=XL2.rearrange("p a b -> p (a b)")))
                sSt.dma(AC.dma_start(out=E["dWF"][:],
                                     in_=WFL.rearrange("p a b -> p (a b)")))
                sSt.dma(AC.dma_start(out=E["dOA"][:],
                                     in_=OACC.rearrange("p a b -> p (a b)")))
                sSt.dma(AC.dma_start(out=E["dMV"][:],
                                     in_=E["MV"].rearrange("p a b -> p (a b)")))
            stBase = sSt.v
            TE.wait_ge(sA.h, sA.v)
            for j in range(8):
                par = j % 2
                TE.wait_ge(sV.h, sV.v)
                if j >= 2:
                    TE.wait_ge(sSt.h, stBase + 16 * (j - 1))
                ps = psA if par == 0 else psB
                psTT = ps[:, 0, :].rearrange("p (c d) -> p c d", d=128)
                for dc in range(DC):
                    inst = TE.transpose(psTT[:, dc, :],
                                        OACC[:, dc, j * 128:(j + 1) * 128], identf[:])
                sT.inc(inst)
                VE.wait_ge(sT.h, sT.v)
                if j >= 2:
                    VE.wait_ge(sSt.h, sSt.v)
                inst = VE.tensor_tensor(out=OTM[:, par, :], in0=ps[:, 0, :],
                                        in1=YR[:, j, :], op=ALU.add)
                sV.inc(inst)
                AC.wait_ge(sV.h, sV.v)
                sSt.dma(AC.dma_start(
                    out=out_d.rearrange("(r p) d -> r p d", p=128)[tp * 8 + j],
                    in_=OTM[:, par, :]))


def kernel(L0_attn_x, L0_attn_w_qkv, L0_attn_b_qkv, L0_attn_w_o, L0_attn_b_o,
           L0_ffn_Wr, L0_ffn_rb, L0_ffn_W1, L0_ffn_W2, L0_ffn_W3,
           L0_ffn_Ws1, L0_ffn_Ws2, L0_ffn_Ws3):
    x = np.ascontiguousarray(np.asarray(L0_attn_x, dtype=np.float32))
    w_qkv = np.asarray(L0_attn_w_qkv, dtype=np.float32)
    b_qkv = np.asarray(L0_attn_b_qkv, dtype=np.float32)
    w_o = np.asarray(L0_attn_w_o, dtype=np.float32)
    b_o = np.asarray(L0_attn_b_o, dtype=np.float32)
    wT = np.ascontiguousarray(w_qkv.T)
    arrs = dict(
        wqT=np.ascontiguousarray(wT[:, 0:1024]),
        wkT=np.ascontiguousarray(wT[:, 1024:2048]),
        wvT=np.ascontiguousarray(wT[:, 2048:3072]),
        woT=np.ascontiguousarray(w_o.T),
        bq=np.ascontiguousarray(b_qkv[0:1024].reshape(8, 128).T),
        bk=np.ascontiguousarray(b_qkv[1024:2048].reshape(8, 128).T),
        bv=np.ascontiguousarray(b_qkv[2048:3072].reshape(1, 1024)),
        bo=np.ascontiguousarray(b_o.reshape(8, 128).T),
        Wr=np.ascontiguousarray(np.asarray(L0_ffn_Wr, dtype=np.float32)),
        rbt=np.ascontiguousarray(
            np.tile(np.asarray(L0_ffn_rb, dtype=np.float32).reshape(1, 8), (1, 8))),
        W1=np.ascontiguousarray(np.asarray(L0_ffn_W1, dtype=np.float32)),
        W2=np.ascontiguousarray(np.asarray(L0_ffn_W2, dtype=np.float32)),
        W3=np.ascontiguousarray(np.asarray(L0_ffn_W3, dtype=np.float32)),
        Ws1=np.ascontiguousarray(np.asarray(L0_ffn_Ws1, dtype=np.float32)),
        Ws2=np.ascontiguousarray(np.asarray(L0_ffn_Ws2, dtype=np.float32)),
        Ws3=np.ascontiguousarray(np.asarray(L0_ffn_Ws3, dtype=np.float32)),
        identb=np.eye(128, dtype=ml_dtypes.bfloat16),
        identf=np.eye(128, dtype=np.float32),
        sele=np.repeat(np.eye(8), 128, axis=1).astype(ml_dtypes.bfloat16),
        hmask=np.repeat(np.eye(2), 64, axis=0).astype(np.float32),
    )
    nc = build_nc()
    in_maps = []
    for c in range(8):
        b, hf = c // 2, c % 2
        m = dict(arrs)
        m["x_kv"] = np.ascontiguousarray(x[b])
        m["x_own"] = np.ascontiguousarray(x[b, hf * 2048:(hf + 1) * 2048])
        in_maps.append(m)
    res = run_bass_kernel_spmd(nc, in_maps, core_ids=list(range(8)))
    out = np.empty((4, 4096, 1024), dtype=np.float32)
    for c in range(8):
        b, hf = c // 2, c % 2
        out[b, hf * 2048:(hf + 1) * 2048] = res.results[c]["out"]
    return out


# version 06
# speedup vs baseline: 1.0050x; 1.0050x over previous
import numpy as np
import ml_dtypes
from contextlib import ExitStack
import concourse.bass as bass
import concourse.mybir as mybir
from concourse.alu_op_type import AluOpType as ALU
from concourse.bass_utils import run_bass_kernel_spmd

F32 = mybir.dt.float32
BF16 = mybir.dt.bfloat16
AF = mybir.ActivationFunctionType
AX = mybir.AxisListType

D = 1024
DC = 8
NP = 8
MKV = 4096
TOWN = 2048
QT = 512
NQT = TOWN // QT
NCH = MKV // 128
HID = 2048
HC = 16
NE = 8
TP = 1024
NTP = TOWN // TP
SCALE = 0.03125
EPS = 1e-5

DBG_NSUPER = 8
DBG_NTILE = NQT
DBG_NPAIR = NP
DBG_1A = 9


class Sem:
    def __init__(self, nc, name):
        self.h = nc.alloc_semaphore(name)
        self.v = 0

    def inc(self, inst, n=1):
        inst.then_inc(self.h, n)
        self.v += n
        return self.v

    def dma(self, inst):
        inst.then_inc(self.h, 16)
        self.v += 16
        return self.v


def build_nc(y_external=False, phases=(1, 2), y_input=False, dbg2=False):
    nc = bass.Bass("TRN2", target_bir_lowering=False)
    E = {}

    for nm, sh in [("x_kv", [MKV, D]), ("x_own", [TOWN, D]),
                   ("wqT", [D, D]), ("wkT", [D, D]), ("wvT", [D, D]), ("woT", [D, D]),
                   ("bq", [128, NP]), ("bk", [128, NP]), ("bv", [1, D]), ("bo", [128, DC]),
                   ("Wr", [D, NE]), ("rbt", [1, 64]),
                   ("W1", [NE, D, HID]), ("W2", [NE, HID, D]), ("W3", [NE, D, HID]),
                   ("Ws1", [D, HID]), ("Ws2", [HID, D]), ("Ws3", [D, HID]),
                   ("identf", [128, 128])]:
        E[nm] = nc.dram_tensor(nm, sh, F32, kind="ExternalInput")
    E["identb"] = nc.dram_tensor("identb", [128, 128], BF16, kind="ExternalInput")
    E["sele"] = nc.dram_tensor("sele", [8, NE * 128], BF16, kind="ExternalInput")
    E["hmask"] = nc.dram_tensor("hmask", [128, 2], F32, kind="ExternalInput")
    E["out_d"] = nc.dram_tensor("out", [TOWN, D], F32, kind="ExternalOutput")
    E["y_scr"] = nc.dram_tensor("y_scr", [TOWN, D], F32,
                                kind="ExternalInput" if y_input else
                                ("ExternalOutput" if y_external else "Internal"))
    E["dbg2"] = dbg2
    if dbg2:
        E["dXL"] = nc.dram_tensor("dXL", [128, DC * 1024], BF16, kind="ExternalOutput")
        E["dWF"] = nc.dram_tensor("dWF", [128, 64], F32, kind="ExternalOutput")
        E["dOA"] = nc.dram_tensor("dOA", [128, DC * 1024], F32, kind="ExternalOutput")
        E["dMV"] = nc.dram_tensor("dMV", [128, 16], F32, kind="ExternalOutput")

    E["identb_s"] = nc.alloc_sbuf_tensor("identb_s", [128, 128], BF16)
    E["identf_s"] = nc.alloc_sbuf_tensor("identf_s", [128, 128], F32)
    E["ones32"] = nc.alloc_sbuf_tensor("ones32", [1, 128], F32)
    E["bq_s"] = nc.alloc_sbuf_tensor("bq_s", [128, NP], F32)
    E["bk_s"] = nc.alloc_sbuf_tensor("bk_s", [128, NP], F32)
    E["bo_s"] = nc.alloc_sbuf_tensor("bo_s", [128, DC], F32)
    E["bvrow"] = nc.alloc_sbuf_tensor("bvrow", [1, D], BF16)
    E["KA8"] = nc.alloc_sbuf_tensor("KA8", [128, NP], F32)
    E["hmask_s"] = nc.alloc_sbuf_tensor("hmask_s", [128, 2], F32)
    E["ST6"] = nc.alloc_sbuf_tensor("ST6", [128, 8, 12], F32)
    E["MV"] = nc.alloc_sbuf_tensor("MV", [128, 8, 2], F32)
    E["LNV"] = nc.alloc_sbuf_tensor("LNV", [128, 8], F32)
    E["EPSV"] = nc.alloc_sbuf_tensor("EPSV", [128, 1], F32)
    E["onesb16"] = nc.alloc_sbuf_tensor("onesb16", [1, 128], BF16)
    E["RSTD"] = nc.alloc_sbuf_tensor("RSTD", [128, 8], F32)

    for s in ("sX", "sSt", "sG", "sT", "sV", "sA", "sWF"):
        E[s] = Sem(nc, s)
    E["sW1"] = [Sem(nc, f"sW1_{i}") for i in range(4)]
    E["sW3"] = [Sem(nc, f"sW3_{i}") for i in range(4)]
    E["sW2"] = [Sem(nc, f"sW2_{i}") for i in range(2)]
    sX, sSt, sG, sT, sV, sA = (E[k] for k in ("sX", "sSt", "sG", "sT", "sV", "sA"))
    TE, VE, AC, GP, SY = nc.tensor, nc.vector, nc.scalar, nc.gpsimd, nc.sync
    E["GPQ"] = GP

    def ln_stats(xt_ap, n_tsub):
        ST6, MV, LNV, RSTD = E["ST6"], E["MV"], E["LNV"], E["RSTD"]
        for j in range(n_tsub):
            VE.bn_stats(ST6[:, j, 0:6], xt_ap[:, j, 0:512])
            VE.bn_stats(ST6[:, j, 6:12], xt_ap[:, j, 512:1024])
        VE.drain()
        for j in range(n_tsub):
            VE.bn_aggr(MV[:, j, :], ST6[:, j, :])
        sV.inc(VE.drain())
        AC.wait_ge(sV.h, sV.v)
        AC.activation(LNV[:, 0:n_tsub], MV[:, 0:n_tsub, 1], AF.Ln,
                      bias=E["EPSV"][:], scale=1.0)
        AC.drain()
        inst = AC.activation(RSTD[:, 0:n_tsub], LNV[:, 0:n_tsub], AF.Exp,
                             bias=0.0, scale=-0.5)
        sA.inc(inst)

    E["ln_stats"] = ln_stats

    with nc.Block():
        for dst, src in [(E["identb_s"], E["identb"]), (E["identf_s"], E["identf"]),
                         (E["bq_s"], E["bq"]), (E["bk_s"], E["bk"]),
                         (E["bo_s"], E["bo"]), (E["hmask_s"], E["hmask"])]:
            sX.dma(SY.dma_start(out=dst[:], in_=src[:]))
        sG.dma(GP.dma_start(out=E["bvrow"][:], in_=E["bv"][:]))
        VE.wait_ge(sX.h, sX.v)
        VE.memset(E["ones32"][:], 1.0)
        VE.memset(E["onesb16"][:], 1.0)
        sV.inc(VE.memset(E["EPSV"][:], EPS))

        if 1 in phases:
            emit_phase1(nc, E)
        nc.all_engine_barrier()
        if 2 in phases:
            emit_phase2(nc, E)
        AC.wait_ge(sSt.h, sSt.v)
    return nc


def emit_phase1(nc, E):
    TE, VE, AC, GP, SY = nc.tensor, nc.vector, nc.scalar, nc.gpsimd, nc.sync
    sX, sSt, sG, sT, sV, sA = (E[k] for k in ("sX", "sSt", "sG", "sT", "sV", "sA"))
    x_kv, x_own, y_scr = E["x_kv"], E["x_own"], E["y_scr"]
    bq, bk, bo = E["bq_s"], E["bk_s"], E["bo_s"]
    identb, identf, ones32 = E["identb_s"], E["identf_s"], E["ones32"]
    KA8 = E["KA8"]
    MV, RSTD = E["MV"], E["RSTD"]
    ln_stats = E["ln_stats"]

    with ExitStack() as _esp1:
        _e0 = _esp1.enter_context
        KF = _e0(nc.sbuf_tensor("KF", [128, NP, MKV], BF16))
        VA = _e0(nc.sbuf_tensor("VA", [128, NCH, 1040], BF16))
        XT = _e0(nc.sbuf_tensor("XT", [128, 4, 1024], F32))
        XLNB = _e0(nc.sbuf_tensor("XLNB", [128, 1024], BF16))
        with ExitStack() as _es1a:
            _e = _es1a.enter_context
            WA = _e(nc.sbuf_tensor("WA", [128, DC, 1024], BF16))
            WB = _e(nc.sbuf_tensor("WB", [128, DC, 1024], BF16))
            XLNF = _e(nc.sbuf_tensor("XLNF", [128, DC, 512], BF16))
            bvr = _e(nc.sbuf_tensor("bvr", [128, D], F32))
            psTb = _e(nc.psum_tensor("psTb", [128, DC, 128], BF16))
            psK = _e(nc.psum_tensor("psK", [128, 2, 512], F32))
            psV = _e(nc.psum_tensor("psV", [128, 2, 512], F32))
            psM = _e(nc.psum_tensor("psM", [128, 512], F32))
            sG.dma(GP.dma_start(out=WA[:],
                                in_=E["wkT"].rearrange("(c p) e -> p c e", p=128)))
            w_arr = sG.dma(GP.dma_start(
                out=WB[:], in_=E["wvT"].rearrange("(c p) e -> p c e", p=128)))
            VE.wait_ge(sX.h, sX.v)
            if DBG_1A < 1:
                return
            for h in range(16):
                inst = VE.memset(VA[:, :, 65 * h + 64: 65 * h + 65], 1.0)
            sV.inc(inst)
            TE.wait_ge(sV.h, sV.v)
            TE.wait_ge(sG.h, sG.v)
            for half in range(2):
                if half > 0:
                    TE.wait_ge(sV.h, sV.v)
                sT.inc(TE.matmul(psM[:], E["onesb16"][:],
                                 E["bvrow"][:, half * 512:(half + 1) * 512],
                                 start=True, stop=True))
                VE.wait_ge(sT.h, sT.v)
                sV.inc(VE.tensor_copy(bvr[:, half * 512:(half + 1) * 512], psM[:]))

            xfree = sV.v
            for s in range(DBG_NSUPER):
                SY.wait_ge(sV.h, xfree)
                sX.dma(SY.dma_start(
                    out=XT[:],
                    in_=x_kv.rearrange("(s t p) d -> s p t d", t=4, p=128)[s]))
                VE.wait_ge(sX.h, sX.v)
                if DBG_1A < 2:
                    continue
                ln_stats(XT, 4)
                VE.wait_ge(sA.h, sA.v)
                for j in range(4):
                    inst = VE.tensor_scalar(XLNB[:], XT[:, j, :], MV[:, j, 0:1],
                                            RSTD[:, j:j + 1], ALU.subtract, ALU.mult)
                    sV.inc(inst)
                    if j == 3:
                        xfree = sV.v
                    if DBG_1A < 3:
                        continue
                    TE.wait_ge(sV.h, sV.v)
                    for dc in range(DC):
                        inst = TE.transpose(psTb[:, dc, :],
                                            XLNB[:, dc * 128:(dc + 1) * 128], identb[:])
                    sT.inc(inst)
                    VE.wait_ge(sT.h, sT.v)
                    inst = VE.tensor_copy(XLNF[:, :, j * 128:(j + 1) * 128], psTb[:])
                    sV.inc(inst)
                if DBG_1A < 4:
                    continue
                TE.wait_ge(sV.h, sV.v)
                if s == 0:
                    TE.wait_ge(sG.h, w_arr)
                evK = sV.v
                for p in range(NP):
                    if p >= 2:
                        TE.wait_ge(sV.h, evK + p - 1)
                    for dc in range(DC):
                        inst = TE.matmul(psK[:, p % 2, :],
                                         WA[:, dc, 128 * p:128 * (p + 1)],
                                         XLNF[:, dc, :],
                                         start=(dc == 0), stop=(dc == DC - 1))
                    sT.inc(inst)
                    VE.wait_ge(sT.h, sT.v)
                    inst = VE.tensor_scalar(KF[:, p, 512 * s:512 * (s + 1)],
                                            psK[:, p % 2, :], bk[:, p:p + 1],
                                            None, ALU.add)
                    sV.inc(inst)
                evV = sV.v
                if DBG_1A < 5:
                    continue
                for ms in range(4):
                    ch = 4 * s + ms
                    for half in range(2):
                        idx = ms * 2 + half
                        if idx >= 2:
                            TE.wait_ge(sV.h, evV + idx - 1)
                        for dc in range(DC):
                            inst = TE.matmul(psV[:, idx % 2, :],
                                             XLNF[:, dc, ms * 128:(ms + 1) * 128],
                                             WB[:, dc, half * 512:(half + 1) * 512],
                                             start=(dc == 0), stop=(dc == DC - 1))
                        sT.inc(inst)
                        VE.wait_ge(sT.h, sT.v)
                        inst = VE.tensor_tensor(
                            out=VA[:, ch, 520 * half:520 * half + 520]
                                .rearrange("p (h cc) -> p h cc", cc=65)[:, :, 0:64],
                            in0=psV[:, idx % 2, :].rearrange("p (h cc) -> p h cc", cc=64),
                            in1=bvr[:, half * 512:(half + 1) * 512]
                                .rearrange("p (h cc) -> p h cc", cc=64),
                            op=ALU.add)
                        sV.inc(inst)
            if DBG_1A < 6:
                return
            pass

        nc.all_engine_barrier()

        with ExitStack() as _es1b:
            _e = _es1b.enter_context
            Wsh = _e(nc.sbuf_tensor("Wsh", [128, DC, 1024], BF16))
            XLNF2 = _e(nc.sbuf_tensor("XLNF2", [128, DC, 512], BF16))
            QFm = _e(nc.sbuf_tensor("QFm", [128, NP, 512], BF16))
            PT = _e(nc.sbuf_tensor("PT", [128, 2, 2, 1024], BF16))
            HDS = _e(nc.sbuf_tensor("HDS", [128, NP, 512], BF16))
            tmpB = _e(nc.sbuf_tensor("tmpB", [64, 512], BF16))
            LROW = _e(nc.sbuf_tensor("LROW", [1, 2, 512], BF16))
            LINV = _e(nc.sbuf_tensor("LINV", [64, 512], BF16))
            PRJ = _e(nc.sbuf_tensor("PRJ", [128, 128], F32))
            YTM = _e(nc.sbuf_tensor("YTM", [128, 1024], F32))
            CMS = _e(nc.sbuf_tensor("CMS", [128, 16], F32))
            CM1 = _e(nc.sbuf_tensor("CM1", [128, 1], F32))
            Gt = _e(nc.sbuf_tensor("Gt", [1, 4], F32))
            NEGG = _e(nc.sbuf_tensor("NEGG", [128, 1], F32))
            psS_A = _e(nc.psum_tensor("psS_A", [128, 2, 512], F32))
            psS_B = _e(nc.psum_tensor("psS_B", [128, 2, 512], F32))
            psPV_A = _e(nc.psum_tensor("psPV_A", [128, 512], F32))
            psPV_B = _e(nc.psum_tensor("psPV_B", [128, 512], F32))
            psM1 = _e(nc.psum_tensor("psM1", [128, 512], F32))
            psTb2 = _e(nc.psum_tensor("psTb2", [128, DC, 128], BF16))
            woDone = sT.v
            yDone = sSt.v
            for t in range(DBG_NTILE):
                if t > 0:
                    GP.wait_ge(sT.h, woDone)
                wq_arr = sG.dma(GP.dma_start(
                    out=Wsh[:], in_=E["wqT"].rearrange("(c p) e -> p c e", p=128)))
                if t > 0:
                    SY.wait_ge(sSt.h, yDone)
                sX.dma(SY.dma_start(
                    out=XT[:],
                    in_=x_own.rearrange("(s t p) d -> s p t d", t=4, p=128)[t]))
                VE.wait_ge(sX.h, sX.v)
                ln_stats(XT, 4)
                VE.wait_ge(sA.h, sA.v)
                for j in range(4):
                    inst = VE.tensor_scalar(XLNB[:], XT[:, j, :], MV[:, j, 0:1],
                                            RSTD[:, j:j + 1], ALU.subtract, ALU.mult)
                    sV.inc(inst)
                    TE.wait_ge(sV.h, sV.v)
                    for dc in range(DC):
                        inst = TE.transpose(psTb2[:, dc, :],
                                            XLNB[:, dc * 128:(dc + 1) * 128], identb[:])
                    sT.inc(inst)
                    VE.wait_ge(sT.h, sT.v)
                    inst = VE.tensor_copy(XLNF2[:, :, j * 128:(j + 1) * 128], psTb2[:])
                    sV.inc(inst)
                TE.wait_ge(sV.h, sV.v)
                TE.wait_ge(sG.h, wq_arr)
                evQ = sV.v
                for p in range(NP):
                    if p >= 1:
                        TE.wait_ge(sV.h, evQ + p)
                    for dc in range(DC):
                        inst = TE.matmul(psM1[:], Wsh[:, dc, 128 * p:128 * (p + 1)],
                                         XLNF2[:, dc, :],
                                         start=(dc == 0), stop=(dc == DC - 1))
                    sT.inc(inst)
                    qDone = sT.v
                    VE.wait_ge(sT.h, sT.v)
                    inst = VE.tensor_scalar(QFm[:, p, :], psM1[:], bq[:, p:p + 1],
                                            None, ALU.add)
                    sV.inc(inst)
                GP.wait_ge(sT.h, qDone)
                wo_arr = sG.dma(GP.dma_start(
                    out=Wsh[:], in_=E["woT"].rearrange("(c p) e -> p c e", p=128)))
                epiV = sV.v
                shiftSt = sSt.v
                for p in range(DBG_NPAIR):
                    TE.wait_ge(sV.h, sV.v)
                    smBase = sV.v
                    for idx, cj in enumerate(range(0, NCH, 4)):
                        if idx >= 2:
                            TE.wait_ge(sV.h, smBase + idx - 1)
                        TE.matmul(psS_A[:, idx % 2, :],
                                  KF[0:64, p, 128 * cj:128 * (cj + 1)],
                                  QFm[0:64, p, :], start=True, stop=True,
                                  tile_position=(0, 0))
                        inst = TE.matmul(psS_B[:, idx % 2, :],
                                         KF[64:128, p, 128 * cj:128 * (cj + 1)],
                                         QFm[64:128, p, :], start=True, stop=True,
                                         tile_position=(64, 0))
                        sT.inc(inst)
                        VE.wait_ge(sT.h, sT.v)
                        sampA = psS_A[:, idx % 2, :].rearrange(
                            "p (a b) -> p a b", b=4)[:, :, 0]
                        sampB = psS_B[:, idx % 2, :].rearrange(
                            "p (a b) -> p a b", b=4)[:, :, 0]
                        VE.reduce_max(CMS[:, 2 * idx:2 * idx + 1], sampA, axis=AX.X)
                        inst = VE.reduce_max(CMS[:, 2 * idx + 1:2 * idx + 2], sampB,
                                             axis=AX.X)
                        sV.inc(inst)
                    VE.drain()
                    VE.reduce_max(CM1[:], CMS[:], axis=AX.X)
                    sV.inc(VE.drain())
                    TE.wait_ge(sV.h, sV.v)
                    sT.inc(TE.transpose(psM1[0:1, 0:128], CM1[:], identf[:]))
                    VE.wait_ge(sT.h, sT.v)
                    VE.reduce_max(Gt[:, 0:1], psM1[0:1, 0:128], axis=AX.X)
                    VE.drain()
                    VE.tensor_scalar(Gt[:, 1:2], Gt[:, 0:1], SCALE, 20.0,
                                     ALU.mult, ALU.add)
                    VE.drain()
                    VE.tensor_scalar(Gt[:, 2:3], Gt[:, 1:2], -1.0, None, ALU.mult)
                    sV.inc(VE.drain())
                    TE.wait_ge(sV.h, sV.v)
                    sT.inc(TE.matmul(psM1[:, 0:1], ones32[:], Gt[:, 2:3],
                                     start=True, stop=True))
                    VE.wait_ge(sT.h, sT.v)
                    negg_v = sV.inc(VE.tensor_copy(NEGG[:], psM1[:, 0:1]))
                    AC.wait_ge(sV.h, negg_v)
                    TE.wait_ge(sV.h, negg_v)
                    tBase = sT.v
                    aBase = sA.v
                    for k in range(17):
                        if k < 16:
                            if k >= 1:
                                TE.wait_ge(sA.h, aBase + 2 * k)
                            for u in range(2):
                                j = 2 * k + u
                                sT.inc(TE.matmul(
                                    psS_A[:, u, :], KF[0:64, p, 128 * j:128 * (j + 1)],
                                    QFm[0:64, p, :], start=True, stop=True,
                                    tile_position=(0, 0)))
                            for u in range(2):
                                j = 2 * k + u
                                sT.inc(TE.matmul(
                                    psS_B[:, u, :], KF[64:128, p, 128 * j:128 * (j + 1)],
                                    QFm[64:128, p, :], start=True, stop=True,
                                    tile_position=(64, 0)))
                        if k >= 1:
                            kk = k - 1
                            TE.wait_ge(sA.h, aBase + 2 * kk + 2)
                            if p > 0 and kk == 0:
                                TE.wait_ge(sV.h, epiV)
                            for u in range(2):
                                j = 2 * kk + u
                                sT.inc(TE.matmul(
                                    psPV_A[0:65, :],
                                    VA[:, j, 65 * (2 * p):65 * (2 * p) + 65],
                                    PT[:, 0, kk % 2, u * 512:(u + 1) * 512],
                                    start=(j == 0), stop=(j == NCH - 1)))
                                sT.inc(TE.matmul(
                                    psPV_B[0:65, :],
                                    VA[:, j, 65 * (2 * p + 1):65 * (2 * p + 1) + 65],
                                    PT[:, 1, kk % 2, u * 512:(u + 1) * 512],
                                    start=(j == 0), stop=(j == NCH - 1)))
                        if k < 16:
                            AC.wait_ge(sT.h,
                                       tBase + (2 if k == 0 else 4 + 8 * (k - 1) + 2))
                            sA.inc(AC.activation(
                                PT[:, 0, k % 2, :],
                                psS_A.rearrange("p a b -> p (a b)"),
                                AF.Exp, bias=NEGG[:], scale=SCALE))
                            AC.wait_ge(sT.h,
                                       tBase + (4 if k == 0 else 4 + 8 * (k - 1) + 4))
                            sA.inc(AC.activation(
                                PT[:, 1, k % 2, :],
                                psS_B.rearrange("p a b -> p (a b)"),
                                AF.Exp, bias=NEGG[:], scale=SCALE))
                    VE.wait_ge(sT.h, sT.v)
                    with nc.allow_low_precision(
                            reason="1/l stored bf16; heads are bf16 anyway"):
                        VE.reciprocal(LROW[:, 0, :], psPV_A[64:65, :])
                        sV.inc(VE.reciprocal(LROW[:, 1, :], psPV_B[64:65, :]))
                    TE.wait_ge(sV.h, sV.v)
                    sT.inc(TE.matmul(psM1[0:64, :], E["onesb16"][0:1, 0:64],
                                     LROW[:, 0, :], start=True, stop=True))
                    VE.wait_ge(sT.h, sT.v)
                    VE.tensor_scalar(LINV[:], psM1[0:64, :], 0.0, None, ALU.add)
                    VE.drain()
                    inst = VE.tensor_tensor(out=HDS[0:64, p, :], in0=psPV_A[0:64, :],
                                            in1=LINV[:], op=ALU.mult)
                    sV.inc(inst)
                    TE.wait_ge(sV.h, sV.v)
                    sT.inc(TE.matmul(psM1[0:64, :], E["onesb16"][0:1, 0:64],
                                     LROW[:, 1, :], start=True, stop=True))
                    VE.wait_ge(sT.h, sT.v)
                    VE.tensor_scalar(LINV[:], psM1[0:64, :], 0.0, None, ALU.add)
                    VE.drain()
                    if p > 0:
                        VE.wait_ge(sSt.h, shiftSt + 16 * p)
                    inst = VE.tensor_tensor(out=tmpB[:], in0=psPV_B[0:64, :],
                                            in1=LINV[:], op=ALU.mult)
                    epiV = sV.inc(inst)
                    AC.wait_ge(sV.h, epiV)
                    sSt.dma(AC.dma_start(out=HDS[64:128, p, :], in_=tmpB[:]))

                TE.wait_ge(sSt.h, sSt.v)
                TE.wait_ge(sG.h, wo_arr)
                ytmV = sV.v
                for j in range(4):
                    psT = psS_A if j % 2 == 0 else psS_B
                    psTbig = psT.rearrange("p a b -> p (a b)")
                    psTv = psTbig.rearrange("p (c d) -> p c d", d=128)
                    if j >= 2:
                        TE.wait_ge(sV.h, ytmV)
                    evW = sV.v
                    for dc in range(DC):
                        wo_ps = psM1[:, 0:128] if dc % 2 == 0 else psPV_A[:, 0:128]
                        if dc >= 2:
                            TE.wait_ge(sV.h, evW + dc - 1)
                        for rc in range(DC):
                            inst = TE.matmul(wo_ps,
                                             Wsh[:, rc, 128 * dc:128 * (dc + 1)],
                                             HDS[:, rc, j * 128:(j + 1) * 128],
                                             start=(rc == 0), stop=(rc == DC - 1))
                        sT.inc(inst)
                        VE.wait_ge(sT.h, sT.v)
                        inst = VE.tensor_scalar(PRJ[:], wo_ps,
                                                bo[:, dc:dc + 1], None, ALU.add)
                        sV.inc(inst)
                        TE.wait_ge(sV.h, sV.v)
                        sT.inc(TE.transpose(psTv[:, dc, :], PRJ[:], identf[:]))
                    VE.wait_ge(sT.h, sT.v)
                    if j > 0:
                        VE.wait_ge(sSt.h, sSt.v)
                    VE.tensor_scalar(XLNB[:], XT[:, j, :], MV[:, j, 0:1],
                                     RSTD[:, j:j + 1], ALU.subtract, ALU.mult)
                    VE.tensor_tensor(out=YTM[:], in0=psTbig, in1=XT[:, j, :],
                                     op=ALU.add)
                    inst = VE.tensor_tensor(out=YTM[:], in0=YTM[:], in1=XLNB[:],
                                            op=ALU.add)
                    ytmV = sV.inc(inst)
                    AC.wait_ge(sV.h, ytmV)
                    sSt.dma(AC.dma_start(
                        out=y_scr.rearrange("(r p) d -> r p d", p=128)[4 * t + j],
                        in_=YTM[:]))
                woDone = sT.v
                yDone = sSt.v
            SY.wait_ge(sSt.h, sSt.v)


def emit_phase2(nc, E):
    TE, VE, AC, GP, SY = nc.tensor, nc.vector, nc.scalar, nc.gpsimd, nc.sync
    sX, sSt, sG, sT, sV, sA, sWF = (E[k] for k in
                                    ("sX", "sSt", "sG", "sT", "sV", "sA", "sWF"))
    y_scr, out_d = E["y_scr"], E["out_d"]
    identf, ones32 = E["identf_s"], E["ones32"]
    MV, RSTD = E["MV"], E["RSTD"]
    ln_stats = E["ln_stats"]

    def w1pos(e, hc):
        return e * 40 + 2 * hc

    def w3pos(e, hc):
        return e * 40 + 2 * hc + 1

    def w2pos(e, dcc):
        return e * 40 + 32 + dcc

    NPOS = (NE + 1) * 40

    with ExitStack() as _es2:
        _e2 = _es2.enter_context
        YR = _e2(nc.sbuf_tensor("YR", [128, 8, 1024], F32))
        XL2 = _e2(nc.sbuf_tensor("XL2", [128, DC, 1024], BF16))
        XL32 = _e2(nc.sbuf_tensor("XL32", [128, 1024], F32))
        X32R = _e2(nc.sbuf_tensor("X32R", [128, 2, DC, 128], F32))
        HW = _e2(nc.sbuf_tensor("HW", [128, HC, 1024], BF16))
        OACC = _e2(nc.sbuf_tensor("OACC", [128, DC, 1024], F32))
        WS1 = _e2(nc.sbuf_tensor("WS1", [128, 4, DC, 128], BF16))
        WS3 = _e2(nc.sbuf_tensor("WS3", [128, 4, DC, 128], BF16))
        WS2 = _e2(nc.sbuf_tensor("WS2", [128, 2, HC, 128], BF16))
        WREP = _e2(nc.sbuf_tensor("WREP", [128, NE, 1024], BF16))
        SILB = _e2(nc.sbuf_tensor("SILB", [128, 2, 1024], BF16))
        HTMP = _e2(nc.sbuf_tensor("HTMP", [128, 2, 1024], BF16))
        Wr32 = _e2(nc.sbuf_tensor("Wr32", [128, DC, 8], F32))
        RBR = _e2(nc.sbuf_tensor("RBR", [128, 64], F32))
        LG = _e2(nc.sbuf_tensor("LG", [128, 8, 8], F32))
        A8 = _e2(nc.sbuf_tensor("A8", [128, 8, 8], F32))
        MK1 = _e2(nc.sbuf_tensor("MK1", [128, 8, 8], F32))
        MK2 = _e2(nc.sbuf_tensor("MK2", [128, 8, 8], F32))
        T8 = _e2(nc.sbuf_tensor("T8", [128, 8, 8], F32))
        WFL = _e2(nc.sbuf_tensor("WFL", [128, 8, 8], F32))
        M12 = _e2(nc.sbuf_tensor("M12", [128, 8, 2], F32))
        L12 = _e2(nc.sbuf_tensor("L12", [128, 8, 2], F32))
        E12 = _e2(nc.sbuf_tensor("E12", [128, 8, 2], F32))
        NEGM = _e2(nc.sbuf_tensor("NEGM", [128, 8], F32))
        ZZ = _e2(nc.sbuf_tensor("ZZ", [128, 8], F32))
        ZI = _e2(nc.sbuf_tensor("ZI", [128, 8], F32))
        WC = _e2(nc.sbuf_tensor("WC", [128, 8, 2], F32))
        WFT = _e2(nc.sbuf_tensor("WFT", [8, 1024], BF16))
        SELE = _e2(nc.sbuf_tensor("SELE", [8, NE, 128], BF16))
        OTM = _e2(nc.sbuf_tensor("OTM", [128, 2, 1024], F32))
        psA = _e2(nc.psum_tensor("psA", [128, 2, 1024], F32))
        psB = _e2(nc.psum_tensor("psB", [128, 2, 1024], F32))
        sX.dma(SY.dma_start(out=Wr32[:],
                            in_=E["Wr"].rearrange("(c p) e -> p c e", p=128)))
        sX.dma(SY.dma_start(out=RBR[0:1, :], in_=E["rbt"][:]))
        TE.wait_ge(sX.h, sX.v)
        sT.inc(TE.matmul(psA[:, 0, 0:64], ones32[:], RBR[0:1, :],
                         start=True, stop=True))
        VE.wait_ge(sT.h, sT.v)
        sV.inc(VE.tensor_copy(RBR[:, :], psA[:, 0, 0:64]))
        sX.dma(SY.dma_start(out=SELE.rearrange("p a b -> p (a b)"), in_=E["sele"][:]))

        for tp in range(NTP):
            wfBase = sWF.v
            arrival = {}
            for e in range(NE + 1):
                for hc in range(HC):
                    for (WT, src, pos, ring, sems, kind) in (
                        (WS1, E["W1"][e] if e < NE else E["Ws1"], w1pos(e, hc), 4,
                         E["sW1"], "w1"),
                        (WS3, E["W3"][e] if e < NE else E["Ws3"], w3pos(e, hc), 4,
                         E["sW3"], "w3"),
                    ):
                        prev_hc = hc - ring
                        if prev_hc >= 0:
                            GP.wait_ge(sWF.h, wfBase + pos - 2 * ring + 1)
                        elif e > 0:
                            prev_pos = (e - 1) * 40 + 2 * (HC + prev_hc) + (pos % 2)
                            GP.wait_ge(sWF.h, wfBase + prev_pos + 1)
                        elif tp > 0:
                            GP.wait_ge(sWF.h, wfBase)
                        sl = sems[hc % 4]
                        sl.dma(GP.dma_start(
                            out=WT[:, hc % 4, :, :],
                            in_=src.rearrange("(c p) h -> p c h", p=128)
                                  [:, :, hc * 128:(hc + 1) * 128]))
                        arrival[(kind, e, hc)] = (sl, sl.v)
                for dcc in range(DC):
                    pos = w2pos(e, dcc)
                    prev_dcc = dcc - 2
                    if prev_dcc >= 0:
                        GP.wait_ge(sWF.h, wfBase + pos - 2 + 1)
                    elif e > 0:
                        prev_pos = (e - 1) * 40 + 32 + (DC + prev_dcc)
                        GP.wait_ge(sWF.h, wfBase + prev_pos + 1)
                    elif tp > 0:
                        GP.wait_ge(sWF.h, wfBase)
                    sl = E["sW2"][dcc % 2]
                    sl.dma(GP.dma_start(
                        out=WS2[:, dcc % 2, :, :],
                        in_=(E["W2"][e] if e < NE else E["Ws2"])
                            .rearrange("(c p) d -> p c d", p=128)
                            [:, :, dcc * 128:(dcc + 1) * 128]))
                    arrival[("w2", e, dcc)] = (sl, sl.v)

            SY.wait_ge(sSt.h, sSt.v)
            SY.wait_ge(sV.h, sV.v)
            sX.dma(SY.dma_start(
                out=YR[:],
                in_=y_scr.rearrange("(q s p) d -> q p s d", s=8, p=128)[tp]))
            VE.wait_ge(sX.h, sX.v)
            ln_stats(YR, 8)
            VE.wait_ge(sA.h, sA.v)
            rEv = sV.v
            for j in range(8):
                par = j % 2
                inst = VE.tensor_scalar(XL32[:], YR[:, j, :], MV[:, j, 0:1],
                                        RSTD[:, j:j + 1], ALU.subtract, ALU.mult)
                sV.inc(inst)
                TE.wait_ge(sV.h, sV.v)
                if j >= 2:
                    TE.wait_ge(sV.h, rEv)
                psTT = psA[:, par, :].rearrange("p (c d) -> p c d", d=128)
                for dc in range(DC):
                    inst = TE.transpose(psTT[:, dc, :],
                                        XL32[:, dc * 128:(dc + 1) * 128], identf[:])
                sT.inc(inst)
                VE.wait_ge(sT.h, sT.v)
                VE.tensor_scalar(XL2[:, :, j * 128:(j + 1) * 128], psTT[:, :, :],
                                 0.0, None, ALU.add)
                inst = VE.tensor_copy(X32R[:, par, :, :], psTT[:, :, :])
                rEv = sV.inc(inst)
                TE.wait_ge(sV.h, rEv)
                for dc in range(DC):
                    inst = TE.matmul(psB[:, par, 0:8], X32R[:, par, dc, :],
                                     Wr32[:, dc, :],
                                     start=(dc == 0), stop=(dc == DC - 1))
                sT.inc(inst)
                VE.wait_ge(sT.h, sT.v)
                inst = VE.tensor_scalar(LG[:, j, :], psB[:, par, 0:8], SCALE,
                                        None, ALU.mult)
                sV.inc(inst)
            VE.drain()
            VE.tensor_tensor(out=A8.rearrange("p a b -> p (a b)"),
                             in0=LG.rearrange("p a b -> p (a b)"), in1=RBR[:],
                             op=ALU.add)
            VE.drain()
            for j in range(8):
                VE.reduce_max(M12[:, j, 0:1], A8[:, j, :], axis=AX.X)
            VE.drain()
            for j in range(8):
                VE.tensor_scalar(MK1[:, j, :], A8[:, j, :], M12[:, j, 0:1],
                                 None, ALU.is_ge)
            VE.drain()
            VE.tensor_tensor(out=T8.rearrange("p a b -> p (a b)"),
                             in0=LG.rearrange("p a b -> p (a b)"),
                             in1=MK1.rearrange("p a b -> p (a b)"), op=ALU.mult)
            VE.drain()
            for j in range(8):
                VE.reduce_sum(L12[:, j, 0:1], T8[:, j, :], axis=AX.X)
            VE.scalar_tensor_tensor(out=A8.rearrange("p a b -> p (a b)"),
                                    in0=MK1.rearrange("p a b -> p (a b)"),
                                    scalar=-1e30,
                                    in1=A8.rearrange("p a b -> p (a b)"),
                                    op0=ALU.mult, op1=ALU.add)
            VE.drain()
            for j in range(8):
                VE.reduce_max(M12[:, j, 1:2], A8[:, j, :], axis=AX.X)
            VE.drain()
            for j in range(8):
                VE.tensor_scalar(MK2[:, j, :], A8[:, j, :], M12[:, j, 1:2],
                                 None, ALU.is_ge)
            VE.drain()
            VE.tensor_tensor(out=T8.rearrange("p a b -> p (a b)"),
                             in0=LG.rearrange("p a b -> p (a b)"),
                             in1=MK2.rearrange("p a b -> p (a b)"), op=ALU.mult)
            VE.drain()
            for j in range(8):
                VE.reduce_sum(L12[:, j, 1:2], T8[:, j, :], axis=AX.X)
            VE.drain()
            VE.tensor_tensor(out=NEGM[:], in0=L12[:, :, 0], in1=L12[:, :, 1],
                             op=ALU.max)
            VE.drain()
            VE.tensor_scalar(NEGM[:], NEGM[:], -1.0, None, ALU.mult)
            sV.inc(VE.drain())
            AC.wait_ge(sV.h, sV.v)
            for j in range(8):
                AC.activation(E12[:, j, 0:1], L12[:, j, 0:1], AF.Exp,
                              bias=NEGM[:, j:j + 1], scale=1.0)
                inst = AC.activation(E12[:, j, 1:2], L12[:, j, 1:2], AF.Exp,
                                     bias=NEGM[:, j:j + 1], scale=1.0)
            sA.inc(inst)
            VE.wait_ge(sA.h, sA.v)
            VE.tensor_tensor(out=ZZ[:], in0=E12[:, :, 0], in1=E12[:, :, 1], op=ALU.add)
            VE.drain()
            VE.reciprocal(ZI[:], ZZ[:])
            VE.drain()
            VE.tensor_tensor(out=WC[:, :, 0], in0=E12[:, :, 0], in1=ZI[:], op=ALU.mult)
            VE.tensor_tensor(out=WC[:, :, 1], in0=E12[:, :, 1], in1=ZI[:], op=ALU.mult)
            VE.drain()
            for j in range(8):
                VE.tensor_scalar(T8[:, j, :], MK1[:, j, :], WC[:, j, 0:1],
                                 None, ALU.mult)
            VE.drain()
            for j in range(8):
                VE.scalar_tensor_tensor(out=WFL[:, j, :], in0=MK2[:, j, :],
                                        scalar=WC[:, j, 1:2], in1=T8[:, j, :],
                                        op0=ALU.mult, op1=ALU.add)
            sV.inc(VE.drain())
            TE.wait_ge(sV.h, sV.v)
            for j in range(8):
                inst = TE.transpose(psB[0:8, 0, j * 128:(j + 1) * 128],
                                    WFL[:, j, :], identf[:])
            sT.inc(inst)
            VE.wait_ge(sT.h, sT.v)
            sV.inc(VE.tensor_scalar(WFT[:], psB[0:8, 0, :], 0.0, None, ALU.add))
            TE.wait_ge(sV.h, sV.v)
            TE.wait_ge(sX.h, sX.v)
            repEv = sV.v
            for e in range(NE):
                for half in range(2):
                    if e > 0 or half > 0:
                        TE.wait_ge(sV.h, repEv)
                    sT.inc(TE.matmul(psA[:, 0, half * 512:(half + 1) * 512],
                                     SELE[:, e, :],
                                     WFT[:, half * 512:(half + 1) * 512],
                                     start=True, stop=True))
                    VE.wait_ge(sT.h, sT.v)
                    inst = VE.tensor_scalar(WREP[:, e, half * 512:(half + 1) * 512],
                                            psA[:, 0, half * 512:(half + 1) * 512],
                                            0.0, None, ALU.add)
                    repEv = sV.inc(inst)

            for e in range(NE + 1):
                eV = sV.v
                eA = sA.v
                hEv_base = sV.v
                silA_base = sA.v
                for hc in range(HC):
                    par = hc % 2
                    _s, _v = arrival[("w1", e, hc)]
                    TE.wait_ge(_s.h, _v)
                    if hc >= 2:
                        TE.wait_ge(sV.h, hEv_base + hc - 1)
                    else:
                        TE.wait_ge(sV.h, eV)
                    for u in range(2):
                        for dc in range(DC):
                            inst = TE.matmul(psA[:, par, u * 512:(u + 1) * 512],
                                             WS1[:, hc % 4, dc, :],
                                             XL2[:, dc, u * 512:(u + 1) * 512],
                                             start=(dc == 0), stop=(dc == DC - 1))
                    sWF.inc(inst)
                    aReady = sWF.v
                    _s, _v = arrival[("w3", e, hc)]
                    TE.wait_ge(_s.h, _v)
                    if hc >= 2:
                        TE.wait_ge(sA.h, silA_base + hc - 1)
                    elif e > 0 or tp > 0:
                        TE.wait_ge(sA.h, eA)
                    for u in range(2):
                        for dc in range(DC):
                            inst = TE.matmul(psB[:, par, u * 512:(u + 1) * 512],
                                             WS3[:, hc % 4, dc, :],
                                             XL2[:, dc, u * 512:(u + 1) * 512],
                                             start=(dc == 0), stop=(dc == DC - 1))
                    sWF.inc(inst)
                    AC.wait_ge(sWF.h, sWF.v)
                    inst = AC.activation(SILB[:, par, :], psB[:, par, :], AF.Silu,
                                         bias=0.0, scale=1.0)
                    sA.inc(inst)
                    VE.wait_ge(sWF.h, aReady)
                    VE.wait_ge(sA.h, sA.v)
                    VE.tensor_tensor(out=HTMP[:, par, :], in0=psA[:, par, :],
                                     in1=SILB[:, par, :], op=ALU.mult)
                    if e < NE:
                        inst = VE.tensor_tensor(out=HW[:, hc, :], in0=HTMP[:, par, :],
                                                in1=WREP[:, e, :], op=ALU.mult)
                    else:
                        inst = VE.tensor_copy(HW[:, hc, :], HTMP[:, par, :])
                    sV.inc(inst)
                cEv = sV.v
                for dcc in range(DC):
                    par = dcc % 2
                    _s, _v = arrival[("w2", e, dcc)]
                    TE.wait_ge(_s.h, _v)
                    TE.wait_ge(sV.h, hEv_base + HC)
                    if dcc >= 2:
                        TE.wait_ge(sV.h, cEv + dcc - 1)
                    for u in range(2):
                        for hcc in range(HC):
                            inst = TE.matmul(psA[:, par, u * 512:(u + 1) * 512],
                                             WS2[:, dcc % 2, hcc, :],
                                             HW[:, hcc, u * 512:(u + 1) * 512],
                                             start=(hcc == 0), stop=(hcc == HC - 1))
                    sWF.inc(inst)
                    VE.wait_ge(sWF.h, sWF.v)
                    if e == 0:
                        inst = VE.tensor_scalar(OACC[:, dcc, :], psA[:, par, :], 0.0,
                                                None, ALU.add)
                    else:
                        inst = VE.scalar_tensor_tensor(out=OACC[:, dcc, :],
                                                       in0=psA[:, par, :], scalar=0.0,
                                                       in1=OACC[:, dcc, :],
                                                       op0=ALU.add, op1=ALU.add)
                    sV.inc(inst)

            if E["dbg2"] and tp == NTP - 1:
                AC.wait_ge(sV.h, sV.v)
                sSt.dma(AC.dma_start(out=E["dXL"][:],
                                     in_=XL2.rearrange("p a b -> p (a b)")))
                sSt.dma(AC.dma_start(out=E["dWF"][:],
                                     in_=WFL.rearrange("p a b -> p (a b)")))
                sSt.dma(AC.dma_start(out=E["dOA"][:],
                                     in_=OACC.rearrange("p a b -> p (a b)")))
                sSt.dma(AC.dma_start(out=E["dMV"][:],
                                     in_=E["MV"].rearrange("p a b -> p (a b)")))
            stBase = sSt.v
            TE.wait_ge(sA.h, sA.v)
            for j in range(8):
                par = j % 2
                TE.wait_ge(sV.h, sV.v)
                if j >= 2:
                    TE.wait_ge(sSt.h, stBase + 16 * (j - 1))
                ps = psA if par == 0 else psB
                psTT = ps[:, 0, :].rearrange("p (c d) -> p c d", d=128)
                for dc in range(DC):
                    inst = TE.transpose(psTT[:, dc, :],
                                        OACC[:, dc, j * 128:(j + 1) * 128], identf[:])
                sT.inc(inst)
                VE.wait_ge(sT.h, sT.v)
                if j >= 2:
                    VE.wait_ge(sSt.h, sSt.v)
                inst = VE.tensor_tensor(out=OTM[:, par, :], in0=ps[:, 0, :],
                                        in1=YR[:, j, :], op=ALU.add)
                sV.inc(inst)
                AC.wait_ge(sV.h, sV.v)
                sSt.dma(AC.dma_start(
                    out=out_d.rearrange("(r p) d -> r p d", p=128)[tp * 8 + j],
                    in_=OTM[:, par, :]))


def kernel(L0_attn_x, L0_attn_w_qkv, L0_attn_b_qkv, L0_attn_w_o, L0_attn_b_o,
           L0_ffn_Wr, L0_ffn_rb, L0_ffn_W1, L0_ffn_W2, L0_ffn_W3,
           L0_ffn_Ws1, L0_ffn_Ws2, L0_ffn_Ws3):
    x = np.ascontiguousarray(np.asarray(L0_attn_x, dtype=np.float32))
    w_qkv = np.asarray(L0_attn_w_qkv, dtype=np.float32)
    b_qkv = np.asarray(L0_attn_b_qkv, dtype=np.float32)
    w_o = np.asarray(L0_attn_w_o, dtype=np.float32)
    b_o = np.asarray(L0_attn_b_o, dtype=np.float32)
    wT = np.ascontiguousarray(w_qkv.T)
    arrs = dict(
        wqT=np.ascontiguousarray(wT[:, 0:1024]),
        wkT=np.ascontiguousarray(wT[:, 1024:2048]),
        wvT=np.ascontiguousarray(wT[:, 2048:3072]),
        woT=np.ascontiguousarray(w_o.T),
        bq=np.ascontiguousarray(b_qkv[0:1024].reshape(8, 128).T),
        bk=np.ascontiguousarray(b_qkv[1024:2048].reshape(8, 128).T),
        bv=np.ascontiguousarray(b_qkv[2048:3072].reshape(1, 1024)),
        bo=np.ascontiguousarray(b_o.reshape(8, 128).T),
        Wr=np.ascontiguousarray(np.asarray(L0_ffn_Wr, dtype=np.float32)),
        rbt=np.ascontiguousarray(
            np.tile(np.asarray(L0_ffn_rb, dtype=np.float32).reshape(1, 8), (1, 8))),
        W1=np.ascontiguousarray(np.asarray(L0_ffn_W1, dtype=np.float32)),
        W2=np.ascontiguousarray(np.asarray(L0_ffn_W2, dtype=np.float32)),
        W3=np.ascontiguousarray(np.asarray(L0_ffn_W3, dtype=np.float32)),
        Ws1=np.ascontiguousarray(np.asarray(L0_ffn_Ws1, dtype=np.float32)),
        Ws2=np.ascontiguousarray(np.asarray(L0_ffn_Ws2, dtype=np.float32)),
        Ws3=np.ascontiguousarray(np.asarray(L0_ffn_Ws3, dtype=np.float32)),
        identb=np.eye(128, dtype=ml_dtypes.bfloat16),
        identf=np.eye(128, dtype=np.float32),
        sele=np.repeat(np.eye(8), 128, axis=1).astype(ml_dtypes.bfloat16),
        hmask=np.repeat(np.eye(2), 64, axis=0).astype(np.float32),
    )
    nc = build_nc()
    in_maps = []
    for c in range(8):
        b, hf = c // 2, c % 2
        m = dict(arrs)
        m["x_kv"] = np.ascontiguousarray(x[b])
        m["x_own"] = np.ascontiguousarray(x[b, hf * 2048:(hf + 1) * 2048])
        in_maps.append(m)
    res = run_bass_kernel_spmd(nc, in_maps, core_ids=list(range(8)))
    out = np.empty((4, 4096, 1024), dtype=np.float32)
    for c in range(8):
        b, hf = c // 2, c % 2
        out[b, hf * 2048:(hf + 1) * 2048] = res.results[c]["out"]
    return out


# version 08
# speedup vs baseline: 1.0095x; 1.0045x over previous
import numpy as np
import ml_dtypes
from contextlib import ExitStack
import concourse.bass as bass
import concourse.mybir as mybir
from concourse.alu_op_type import AluOpType as ALU
from concourse.bass_utils import run_bass_kernel_spmd

F32 = mybir.dt.float32
BF16 = mybir.dt.bfloat16
AF = mybir.ActivationFunctionType
AX = mybir.AxisListType

D = 1024
DC = 8
NP = 8
MKV = 4096
TOWN = 2048
QT = 512
NQT = TOWN // QT
NCH = MKV // 128
HID = 2048
HC = 16
NE = 8
TP = 1024
NTP = TOWN // TP
SCALE = 0.03125
EPS = 1e-5

DBG_NSUPER = 8
DBG_NTILE = NQT
DBG_NPAIR = NP
DBG_1A = 9


class Sem:
    def __init__(self, nc, name):
        self.h = nc.alloc_semaphore(name)
        self.v = 0

    def inc(self, inst, n=1):
        inst.then_inc(self.h, n)
        self.v += n
        return self.v

    def dma(self, inst):
        inst.then_inc(self.h, 16)
        self.v += 16
        return self.v


def build_nc(y_external=False, phases=(1, 2), y_input=False, dbg2=False):
    nc = bass.Bass("TRN2", target_bir_lowering=False)
    E = {}

    for nm, sh in [("x_kv", [MKV, D]), ("x_own", [TOWN, D]),
                   ("wqT", [D, D]), ("wkT", [D, D]), ("wvT", [D, D]), ("woT", [D, D]),
                   ("bq", [128, NP]), ("bk", [128, NP]), ("bv", [1, D]), ("bo", [128, DC]),
                   ("Wr", [D, NE]), ("rbt", [1, 64]),
                   ("W1", [NE, D, HID]), ("W2", [NE, HID, D]), ("W3", [NE, D, HID]),
                   ("Ws1", [D, HID]), ("Ws2", [HID, D]), ("Ws3", [D, HID]),
                   ("identf", [128, 128])]:
        E[nm] = nc.dram_tensor(nm, sh, F32, kind="ExternalInput")
    E["identb"] = nc.dram_tensor("identb", [128, 128], BF16, kind="ExternalInput")
    E["sele"] = nc.dram_tensor("sele", [8, NE * 128], BF16, kind="ExternalInput")
    E["hmask"] = nc.dram_tensor("hmask", [128, 2], F32, kind="ExternalInput")
    E["out_d"] = nc.dram_tensor("out", [TOWN, D], F32, kind="ExternalOutput")
    E["y_scr"] = nc.dram_tensor("y_scr", [TOWN, D], F32,
                                kind="ExternalInput" if y_input else
                                ("ExternalOutput" if y_external else "Internal"))
    E["dbg2"] = dbg2
    if dbg2:
        E["dXL"] = nc.dram_tensor("dXL", [128, DC * 1024], BF16, kind="ExternalOutput")
        E["dWF"] = nc.dram_tensor("dWF", [128, 64], F32, kind="ExternalOutput")
        E["dOA"] = nc.dram_tensor("dOA", [128, DC * 1024], F32, kind="ExternalOutput")
        E["dMV"] = nc.dram_tensor("dMV", [128, 16], F32, kind="ExternalOutput")

    E["identb_s"] = nc.alloc_sbuf_tensor("identb_s", [128, 128], BF16)
    E["identf_s"] = nc.alloc_sbuf_tensor("identf_s", [128, 128], F32)
    E["ones32"] = nc.alloc_sbuf_tensor("ones32", [1, 128], F32)
    E["bq_s"] = nc.alloc_sbuf_tensor("bq_s", [128, NP], F32)
    E["bk_s"] = nc.alloc_sbuf_tensor("bk_s", [128, NP], F32)
    E["bo_s"] = nc.alloc_sbuf_tensor("bo_s", [128, DC], F32)
    E["bvrow"] = nc.alloc_sbuf_tensor("bvrow", [1, D], BF16)
    E["KA8"] = nc.alloc_sbuf_tensor("KA8", [128, NP], F32)
    E["hmask_s"] = nc.alloc_sbuf_tensor("hmask_s", [128, 2], F32)
    E["ST6"] = nc.alloc_sbuf_tensor("ST6", [128, 8, 12], F32)
    E["MV"] = nc.alloc_sbuf_tensor("MV", [128, 8, 2], F32)
    E["LNV"] = nc.alloc_sbuf_tensor("LNV", [128, 8], F32)
    E["EPSV"] = nc.alloc_sbuf_tensor("EPSV", [128, 1], F32)
    E["onesb16"] = nc.alloc_sbuf_tensor("onesb16", [1, 128], BF16)
    E["RSTD"] = nc.alloc_sbuf_tensor("RSTD", [128, 8], F32)

    for s in ("sX", "sSt", "sG", "sT", "sV", "sA", "sWF"):
        E[s] = Sem(nc, s)
    E["sW1"] = [Sem(nc, f"sW1_{i}") for i in range(4)]
    E["sW3"] = [Sem(nc, f"sW3_{i}") for i in range(4)]
    E["sW2"] = [Sem(nc, f"sW2_{i}") for i in range(2)]
    sX, sSt, sG, sT, sV, sA = (E[k] for k in ("sX", "sSt", "sG", "sT", "sV", "sA"))
    TE, VE, AC, GP, SY = nc.tensor, nc.vector, nc.scalar, nc.gpsimd, nc.sync
    E["GPQ"] = GP

    def ln_stats(xt_ap, n_tsub):
        ST6, MV, LNV, RSTD = E["ST6"], E["MV"], E["LNV"], E["RSTD"]
        for j in range(n_tsub):
            VE.bn_stats(ST6[:, j, 0:6], xt_ap[:, j, 0:512])
            VE.bn_stats(ST6[:, j, 6:12], xt_ap[:, j, 512:1024])
        VE.drain()
        for j in range(n_tsub):
            VE.bn_aggr(MV[:, j, :], ST6[:, j, :])
        sV.inc(VE.drain())
        AC.wait_ge(sV.h, sV.v)
        AC.activation(LNV[:, 0:n_tsub], MV[:, 0:n_tsub, 1], AF.Ln,
                      bias=E["EPSV"][:], scale=1.0)
        AC.drain()
        inst = AC.activation(RSTD[:, 0:n_tsub], LNV[:, 0:n_tsub], AF.Exp,
                             bias=0.0, scale=-0.5)
        sA.inc(inst)

    E["ln_stats"] = ln_stats

    with nc.Block():
        for dst, src in [(E["identb_s"], E["identb"]), (E["identf_s"], E["identf"]),
                         (E["bq_s"], E["bq"]), (E["bk_s"], E["bk"]),
                         (E["bo_s"], E["bo"]), (E["hmask_s"], E["hmask"])]:
            sX.dma(SY.dma_start(out=dst[:], in_=src[:]))
        sG.dma(GP.dma_start(out=E["bvrow"][:], in_=E["bv"][:]))
        VE.wait_ge(sX.h, sX.v)
        VE.memset(E["ones32"][:], 1.0)
        VE.memset(E["onesb16"][:], 1.0)
        sV.inc(VE.memset(E["EPSV"][:], EPS))

        if 1 in phases:
            emit_phase1(nc, E)
        nc.all_engine_barrier()
        if 2 in phases:
            emit_phase2(nc, E)
        AC.wait_ge(sSt.h, sSt.v)
    return nc


def emit_phase1(nc, E):
    TE, VE, AC, GP, SY = nc.tensor, nc.vector, nc.scalar, nc.gpsimd, nc.sync
    sX, sSt, sG, sT, sV, sA = (E[k] for k in ("sX", "sSt", "sG", "sT", "sV", "sA"))
    x_kv, x_own, y_scr = E["x_kv"], E["x_own"], E["y_scr"]
    bq, bk, bo = E["bq_s"], E["bk_s"], E["bo_s"]
    identb, identf, ones32 = E["identb_s"], E["identf_s"], E["ones32"]
    KA8 = E["KA8"]
    MV, RSTD = E["MV"], E["RSTD"]
    ln_stats = E["ln_stats"]

    with ExitStack() as _esp1:
        _e0 = _esp1.enter_context
        KF = _e0(nc.sbuf_tensor("KF", [128, NP, MKV], BF16))
        VA = _e0(nc.sbuf_tensor("VA", [128, NCH, 1040], BF16))
        XT = _e0(nc.sbuf_tensor("XT", [128, 4, 1024], F32))
        XLNB = _e0(nc.sbuf_tensor("XLNB", [128, 1024], BF16))
        with ExitStack() as _es1a:
            _e = _es1a.enter_context
            WA = _e(nc.sbuf_tensor("WA", [128, DC, 1024], BF16))
            WB = _e(nc.sbuf_tensor("WB", [128, DC, 1024], BF16))
            XLNF = _e(nc.sbuf_tensor("XLNF", [128, DC, 512], BF16))
            bvr = _e(nc.sbuf_tensor("bvr", [128, D], F32))
            psTb = _e(nc.psum_tensor("psTb", [128, DC, 128], BF16))
            psK = _e(nc.psum_tensor("psK", [128, 2, 512], F32))
            psV = _e(nc.psum_tensor("psV", [128, 2, 512], F32))
            psM = _e(nc.psum_tensor("psM", [128, 512], F32))
            sG.dma(GP.dma_start(out=WA[:],
                                in_=E["wkT"].rearrange("(c p) e -> p c e", p=128)))
            w_arr = sG.dma(GP.dma_start(
                out=WB[:], in_=E["wvT"].rearrange("(c p) e -> p c e", p=128)))
            VE.wait_ge(sX.h, sX.v)
            if DBG_1A < 1:
                return
            for h in range(16):
                inst = VE.memset(VA[:, :, 65 * h + 64: 65 * h + 65], 1.0)
            sV.inc(inst)
            TE.wait_ge(sV.h, sV.v)
            TE.wait_ge(sG.h, sG.v)
            for half in range(2):
                if half > 0:
                    TE.wait_ge(sV.h, sV.v)
                sT.inc(TE.matmul(psM[:], E["onesb16"][:],
                                 E["bvrow"][:, half * 512:(half + 1) * 512],
                                 start=True, stop=True))
                VE.wait_ge(sT.h, sT.v)
                sV.inc(VE.tensor_copy(bvr[:, half * 512:(half + 1) * 512], psM[:]))

            xfree = sV.v
            for s in range(DBG_NSUPER):
                SY.wait_ge(sV.h, xfree)
                sX.dma(SY.dma_start(
                    out=XT[:],
                    in_=x_kv.rearrange("(s t p) d -> s p t d", t=4, p=128)[s]))
                VE.wait_ge(sX.h, sX.v)
                if DBG_1A < 2:
                    continue
                ln_stats(XT, 4)
                VE.wait_ge(sA.h, sA.v)
                for j in range(4):
                    inst = VE.tensor_scalar(XLNB[:], XT[:, j, :], MV[:, j, 0:1],
                                            RSTD[:, j:j + 1], ALU.subtract, ALU.mult)
                    sV.inc(inst)
                    if j == 3:
                        xfree = sV.v
                    if DBG_1A < 3:
                        continue
                    TE.wait_ge(sV.h, sV.v)
                    for dc in range(DC):
                        inst = TE.transpose(psTb[:, dc, :],
                                            XLNB[:, dc * 128:(dc + 1) * 128], identb[:])
                    sT.inc(inst)
                    VE.wait_ge(sT.h, sT.v)
                    inst = VE.tensor_copy(XLNF[:, :, j * 128:(j + 1) * 128], psTb[:])
                    sV.inc(inst)
                if DBG_1A < 4:
                    continue
                TE.wait_ge(sV.h, sV.v)
                if s == 0:
                    TE.wait_ge(sG.h, w_arr)
                evK = sV.v
                for p in range(NP):
                    if p >= 2:
                        TE.wait_ge(sV.h, evK + p - 1)
                    for dc in range(DC):
                        inst = TE.matmul(psK[:, p % 2, :],
                                         WA[:, dc, 128 * p:128 * (p + 1)],
                                         XLNF[:, dc, :],
                                         start=(dc == 0), stop=(dc == DC - 1))
                    sT.inc(inst)
                    VE.wait_ge(sT.h, sT.v)
                    inst = VE.tensor_scalar(KF[:, p, 512 * s:512 * (s + 1)],
                                            psK[:, p % 2, :], bk[:, p:p + 1],
                                            None, ALU.add)
                    sV.inc(inst)
                evV = sV.v
                if DBG_1A < 5:
                    continue
                for ms in range(4):
                    ch = 4 * s + ms
                    for half in range(2):
                        idx = ms * 2 + half
                        if idx >= 2:
                            TE.wait_ge(sV.h, evV + idx - 1)
                        for dc in range(DC):
                            inst = TE.matmul(psV[:, idx % 2, :],
                                             XLNF[:, dc, ms * 128:(ms + 1) * 128],
                                             WB[:, dc, half * 512:(half + 1) * 512],
                                             start=(dc == 0), stop=(dc == DC - 1))
                        sT.inc(inst)
                        VE.wait_ge(sT.h, sT.v)
                        inst = VE.tensor_tensor(
                            out=VA[:, ch, 520 * half:520 * half + 520]
                                .rearrange("p (h cc) -> p h cc", cc=65)[:, :, 0:64],
                            in0=psV[:, idx % 2, :].rearrange("p (h cc) -> p h cc", cc=64),
                            in1=bvr[:, half * 512:(half + 1) * 512]
                                .rearrange("p (h cc) -> p h cc", cc=64),
                            op=ALU.add)
                        sV.inc(inst)
            if DBG_1A < 6:
                return
            pass

        nc.all_engine_barrier()

        with ExitStack() as _es1b:
            _e = _es1b.enter_context
            Wsh = _e(nc.sbuf_tensor("Wsh", [128, DC, 1024], BF16))
            XLNF2 = _e(nc.sbuf_tensor("XLNF2", [128, DC, 512], BF16))
            QFm = _e(nc.sbuf_tensor("QFm", [128, NP, 512], BF16))
            PT = _e(nc.sbuf_tensor("PT", [128, 2, 2, 1024], BF16))
            HDS = _e(nc.sbuf_tensor("HDS", [128, NP, 512], BF16))
            tmpB = _e(nc.sbuf_tensor("tmpB", [64, 512], BF16))
            LROW = _e(nc.sbuf_tensor("LROW", [1, 2, 512], BF16))
            LINV = _e(nc.sbuf_tensor("LINV", [64, 512], BF16))
            PRJ = _e(nc.sbuf_tensor("PRJ", [128, 128], F32))
            YTM = _e(nc.sbuf_tensor("YTM", [128, 1024], F32))
            CMS = _e(nc.sbuf_tensor("CMS", [128, 16], F32))
            CM1 = _e(nc.sbuf_tensor("CM1", [128, 1], F32))
            Gt = _e(nc.sbuf_tensor("Gt", [1, 4], F32))
            NEGG = _e(nc.sbuf_tensor("NEGG", [128, 1], F32))
            psS_A = _e(nc.psum_tensor("psS_A", [128, 2, 512], F32))
            psS_B = _e(nc.psum_tensor("psS_B", [128, 2, 512], F32))
            psPV_A = _e(nc.psum_tensor("psPV_A", [128, 512], F32))
            psPV_B = _e(nc.psum_tensor("psPV_B", [128, 512], F32))
            psM1 = _e(nc.psum_tensor("psM1", [128, 512], F32))
            psTb2 = _e(nc.psum_tensor("psTb2", [128, DC, 128], BF16))
            woDone = sT.v
            yDone = sSt.v
            for t in range(DBG_NTILE):
                if t > 0:
                    GP.wait_ge(sT.h, woDone)
                wq_arr = sG.dma(GP.dma_start(
                    out=Wsh[:], in_=E["wqT"].rearrange("(c p) e -> p c e", p=128)))
                if t > 0:
                    SY.wait_ge(sSt.h, yDone)
                sX.dma(SY.dma_start(
                    out=XT[:],
                    in_=x_own.rearrange("(s t p) d -> s p t d", t=4, p=128)[t]))
                VE.wait_ge(sX.h, sX.v)
                ln_stats(XT, 4)
                VE.wait_ge(sA.h, sA.v)
                for j in range(4):
                    inst = VE.tensor_scalar(XLNB[:], XT[:, j, :], MV[:, j, 0:1],
                                            RSTD[:, j:j + 1], ALU.subtract, ALU.mult)
                    sV.inc(inst)
                    TE.wait_ge(sV.h, sV.v)
                    for dc in range(DC):
                        inst = TE.transpose(psTb2[:, dc, :],
                                            XLNB[:, dc * 128:(dc + 1) * 128], identb[:])
                    sT.inc(inst)
                    VE.wait_ge(sT.h, sT.v)
                    inst = VE.tensor_copy(XLNF2[:, :, j * 128:(j + 1) * 128], psTb2[:])
                    sV.inc(inst)
                TE.wait_ge(sV.h, sV.v)
                TE.wait_ge(sG.h, wq_arr)
                evQ = sV.v
                for p in range(NP):
                    if p >= 1:
                        TE.wait_ge(sV.h, evQ + p)
                    for dc in range(DC):
                        inst = TE.matmul(psM1[:], Wsh[:, dc, 128 * p:128 * (p + 1)],
                                         XLNF2[:, dc, :],
                                         start=(dc == 0), stop=(dc == DC - 1))
                    sT.inc(inst)
                    qDone = sT.v
                    VE.wait_ge(sT.h, sT.v)
                    inst = VE.tensor_scalar(QFm[:, p, :], psM1[:], bq[:, p:p + 1],
                                            None, ALU.add)
                    sV.inc(inst)
                GP.wait_ge(sT.h, qDone)
                wo_arr = sG.dma(GP.dma_start(
                    out=Wsh[:], in_=E["woT"].rearrange("(c p) e -> p c e", p=128)))
                epiV = sV.v
                shiftSt = sSt.v
                for p in range(DBG_NPAIR):
                    TE.wait_ge(sV.h, sV.v)
                    smBase = sV.v
                    for idx, cj in enumerate(range(0, NCH, 4)):
                        if idx >= 2:
                            TE.wait_ge(sV.h, smBase + idx - 1)
                        TE.matmul(psS_A[:, idx % 2, :],
                                  KF[0:64, p, 128 * cj:128 * (cj + 1)],
                                  QFm[0:64, p, :], start=True, stop=True,
                                  tile_position=(0, 0))
                        inst = TE.matmul(psS_B[:, idx % 2, :],
                                         KF[64:128, p, 128 * cj:128 * (cj + 1)],
                                         QFm[64:128, p, :], start=True, stop=True,
                                         tile_position=(64, 0))
                        sT.inc(inst)
                        VE.wait_ge(sT.h, sT.v)
                        sampA = psS_A[:, idx % 2, :].rearrange(
                            "p (a b) -> p a b", b=4)[:, :, 0]
                        sampB = psS_B[:, idx % 2, :].rearrange(
                            "p (a b) -> p a b", b=4)[:, :, 0]
                        VE.reduce_max(CMS[:, 2 * idx:2 * idx + 1], sampA, axis=AX.X)
                        inst = VE.reduce_max(CMS[:, 2 * idx + 1:2 * idx + 2], sampB,
                                             axis=AX.X)
                        sV.inc(inst)
                    VE.drain()
                    VE.reduce_max(CM1[:], CMS[:], axis=AX.X)
                    sV.inc(VE.drain())
                    TE.wait_ge(sV.h, sV.v)
                    sT.inc(TE.transpose(psM1[0:1, 0:128], CM1[:], identf[:]))
                    VE.wait_ge(sT.h, sT.v)
                    VE.reduce_max(Gt[:, 0:1], psM1[0:1, 0:128], axis=AX.X)
                    VE.drain()
                    VE.tensor_scalar(Gt[:, 1:2], Gt[:, 0:1], SCALE, 20.0,
                                     ALU.mult, ALU.add)
                    VE.drain()
                    VE.tensor_scalar(Gt[:, 2:3], Gt[:, 1:2], -1.0, None, ALU.mult)
                    sV.inc(VE.drain())
                    TE.wait_ge(sV.h, sV.v)
                    sT.inc(TE.matmul(psM1[:, 0:1], ones32[:], Gt[:, 2:3],
                                     start=True, stop=True))
                    VE.wait_ge(sT.h, sT.v)
                    negg_v = sV.inc(VE.tensor_copy(NEGG[:], psM1[:, 0:1]))
                    AC.wait_ge(sV.h, negg_v)
                    TE.wait_ge(sV.h, negg_v)
                    tBase = sT.v
                    aBase = sA.v
                    for k in range(17):
                        if k < 16:
                            if k >= 1:
                                TE.wait_ge(sA.h, aBase + 2 * k)
                            for u in range(2):
                                j = 2 * k + u
                                sT.inc(TE.matmul(
                                    psS_A[:, u, :], KF[0:64, p, 128 * j:128 * (j + 1)],
                                    QFm[0:64, p, :], start=True, stop=True,
                                    tile_position=(0, 0)))
                            for u in range(2):
                                j = 2 * k + u
                                sT.inc(TE.matmul(
                                    psS_B[:, u, :], KF[64:128, p, 128 * j:128 * (j + 1)],
                                    QFm[64:128, p, :], start=True, stop=True,
                                    tile_position=(64, 0)))
                        if k >= 1:
                            kk = k - 1
                            TE.wait_ge(sA.h, aBase + 2 * kk + 2)
                            if p > 0 and kk == 0:
                                TE.wait_ge(sV.h, epiV)
                            for u in range(2):
                                j = 2 * kk + u
                                sT.inc(TE.matmul(
                                    psPV_A[0:65, :],
                                    VA[:, j, 65 * (2 * p):65 * (2 * p) + 65],
                                    PT[:, 0, kk % 2, u * 512:(u + 1) * 512],
                                    start=(j == 0), stop=(j == NCH - 1)))
                                sT.inc(TE.matmul(
                                    psPV_B[0:65, :],
                                    VA[:, j, 65 * (2 * p + 1):65 * (2 * p + 1) + 65],
                                    PT[:, 1, kk % 2, u * 512:(u + 1) * 512],
                                    start=(j == 0), stop=(j == NCH - 1)))
                        if k < 16:
                            AC.wait_ge(sT.h,
                                       tBase + (2 if k == 0 else 4 + 8 * (k - 1) + 2))
                            sA.inc(AC.activation(
                                PT[:, 0, k % 2, :],
                                psS_A.rearrange("p a b -> p (a b)"),
                                AF.Exp, bias=NEGG[:], scale=SCALE))
                            AC.wait_ge(sT.h,
                                       tBase + (4 if k == 0 else 4 + 8 * (k - 1) + 4))
                            sA.inc(AC.activation(
                                PT[:, 1, k % 2, :],
                                psS_B.rearrange("p a b -> p (a b)"),
                                AF.Exp, bias=NEGG[:], scale=SCALE))
                    VE.wait_ge(sT.h, sT.v)
                    with nc.allow_low_precision(
                            reason="1/l stored bf16; heads are bf16 anyway"):
                        VE.reciprocal(LROW[:, 0, :], psPV_A[64:65, :])
                        sV.inc(VE.reciprocal(LROW[:, 1, :], psPV_B[64:65, :]))
                    TE.wait_ge(sV.h, sV.v)
                    sT.inc(TE.matmul(psM1[0:64, :], E["onesb16"][0:1, 0:64],
                                     LROW[:, 0, :], start=True, stop=True))
                    VE.wait_ge(sT.h, sT.v)
                    VE.tensor_scalar(LINV[:], psM1[0:64, :], 0.0, None, ALU.add)
                    VE.drain()
                    inst = VE.tensor_tensor(out=HDS[0:64, p, :], in0=psPV_A[0:64, :],
                                            in1=LINV[:], op=ALU.mult)
                    sV.inc(inst)
                    TE.wait_ge(sV.h, sV.v)
                    sT.inc(TE.matmul(psM1[0:64, :], E["onesb16"][0:1, 0:64],
                                     LROW[:, 1, :], start=True, stop=True))
                    VE.wait_ge(sT.h, sT.v)
                    VE.tensor_scalar(LINV[:], psM1[0:64, :], 0.0, None, ALU.add)
                    VE.drain()
                    if p > 0:
                        VE.wait_ge(sSt.h, shiftSt + 16 * p)
                    inst = VE.tensor_tensor(out=tmpB[:], in0=psPV_B[0:64, :],
                                            in1=LINV[:], op=ALU.mult)
                    epiV = sV.inc(inst)
                    AC.wait_ge(sV.h, epiV)
                    sSt.dma(AC.dma_start(out=HDS[64:128, p, :], in_=tmpB[:]))

                TE.wait_ge(sSt.h, sSt.v)
                TE.wait_ge(sG.h, wo_arr)
                ytmV = sV.v
                for j in range(4):
                    psT = psS_A if j % 2 == 0 else psS_B
                    psTbig = psT.rearrange("p a b -> p (a b)")
                    psTv = psTbig.rearrange("p (c d) -> p c d", d=128)
                    if j >= 2:
                        TE.wait_ge(sV.h, ytmV)
                    evW = sV.v
                    for dc in range(DC):
                        wo_ps = psM1[:, 0:128] if dc % 2 == 0 else psPV_A[:, 0:128]
                        if dc >= 2:
                            TE.wait_ge(sV.h, evW + dc - 1)
                        for rc in range(DC):
                            inst = TE.matmul(wo_ps,
                                             Wsh[:, rc, 128 * dc:128 * (dc + 1)],
                                             HDS[:, rc, j * 128:(j + 1) * 128],
                                             start=(rc == 0), stop=(rc == DC - 1))
                        sT.inc(inst)
                        VE.wait_ge(sT.h, sT.v)
                        inst = VE.tensor_scalar(PRJ[:], wo_ps,
                                                bo[:, dc:dc + 1], None, ALU.add)
                        sV.inc(inst)
                        TE.wait_ge(sV.h, sV.v)
                        sT.inc(TE.transpose(psTv[:, dc, :], PRJ[:], identf[:]))
                    VE.wait_ge(sT.h, sT.v)
                    if j > 0:
                        VE.wait_ge(sSt.h, sSt.v)
                    VE.tensor_scalar(XLNB[:], XT[:, j, :], MV[:, j, 0:1],
                                     RSTD[:, j:j + 1], ALU.subtract, ALU.mult)
                    VE.tensor_tensor(out=YTM[:], in0=psTbig, in1=XT[:, j, :],
                                     op=ALU.add)
                    VE.drain()
                    inst = VE.tensor_tensor(out=YTM[:], in0=YTM[:], in1=XLNB[:],
                                            op=ALU.add)
                    ytmV = sV.inc(inst)
                    AC.wait_ge(sV.h, ytmV)
                    sSt.dma(AC.dma_start(
                        out=y_scr.rearrange("(r p) d -> r p d", p=128)[4 * t + j],
                        in_=YTM[:]))
                woDone = sT.v
                yDone = sSt.v
            SY.wait_ge(sSt.h, sSt.v)


def emit_phase2(nc, E):
    TE, VE, AC, GP, SY = nc.tensor, nc.vector, nc.scalar, nc.gpsimd, nc.sync
    sX, sSt, sG, sT, sV, sA, sWF = (E[k] for k in
                                    ("sX", "sSt", "sG", "sT", "sV", "sA", "sWF"))
    y_scr, out_d = E["y_scr"], E["out_d"]
    identf, ones32 = E["identf_s"], E["ones32"]
    MV, RSTD = E["MV"], E["RSTD"]
    ln_stats = E["ln_stats"]

    def w1pos(e, hc):
        return e * 40 + 2 * hc

    def w3pos(e, hc):
        return e * 40 + 2 * hc + 1

    def w2pos(e, dcc):
        return e * 40 + 32 + dcc

    NPOS = (NE + 1) * 40

    with ExitStack() as _es2:
        _e2 = _es2.enter_context
        YR = _e2(nc.sbuf_tensor("YR", [128, 8, 1024], F32))
        XL2 = _e2(nc.sbuf_tensor("XL2", [128, DC, 1024], BF16))
        XL32 = _e2(nc.sbuf_tensor("XL32", [128, 1024], F32))
        X32R = _e2(nc.sbuf_tensor("X32R", [128, 2, DC, 128], F32))
        HW = _e2(nc.sbuf_tensor("HW", [128, HC, 1024], BF16))
        OACC = _e2(nc.sbuf_tensor("OACC", [128, DC, 1024], F32))
        WS1 = _e2(nc.sbuf_tensor("WS1", [128, 4, DC, 128], BF16))
        WS3 = _e2(nc.sbuf_tensor("WS3", [128, 4, DC, 128], BF16))
        WS2 = _e2(nc.sbuf_tensor("WS2", [128, 2, HC, 128], BF16))
        WREP = _e2(nc.sbuf_tensor("WREP", [128, NE, 1024], BF16))
        SILB = _e2(nc.sbuf_tensor("SILB", [128, 2, 1024], BF16))
        HTMP = _e2(nc.sbuf_tensor("HTMP", [128, 2, 1024], BF16))
        Wr32 = _e2(nc.sbuf_tensor("Wr32", [128, DC, 8], F32))
        RBR = _e2(nc.sbuf_tensor("RBR", [128, 64], F32))
        LG = _e2(nc.sbuf_tensor("LG", [128, 8, 8], F32))
        A8 = _e2(nc.sbuf_tensor("A8", [128, 8, 8], F32))
        MK1 = _e2(nc.sbuf_tensor("MK1", [128, 8, 8], F32))
        MK2 = _e2(nc.sbuf_tensor("MK2", [128, 8, 8], F32))
        T8 = _e2(nc.sbuf_tensor("T8", [128, 8, 8], F32))
        WFL = _e2(nc.sbuf_tensor("WFL", [128, 8, 8], F32))
        M12 = _e2(nc.sbuf_tensor("M12", [128, 8, 2], F32))
        L12 = _e2(nc.sbuf_tensor("L12", [128, 8, 2], F32))
        E12 = _e2(nc.sbuf_tensor("E12", [128, 8, 2], F32))
        NEGM = _e2(nc.sbuf_tensor("NEGM", [128, 8], F32))
        ZZ = _e2(nc.sbuf_tensor("ZZ", [128, 8], F32))
        ZI = _e2(nc.sbuf_tensor("ZI", [128, 8], F32))
        WC = _e2(nc.sbuf_tensor("WC", [128, 8, 2], F32))
        WFT = _e2(nc.sbuf_tensor("WFT", [8, 1024], BF16))
        SELE = _e2(nc.sbuf_tensor("SELE", [8, NE, 128], BF16))
        OTM = _e2(nc.sbuf_tensor("OTM", [128, 2, 1024], F32))
        psA = _e2(nc.psum_tensor("psA", [128, 2, 1024], F32))
        psB = _e2(nc.psum_tensor("psB", [128, 2, 1024], F32))
        sX.dma(SY.dma_start(out=Wr32[:],
                            in_=E["Wr"].rearrange("(c p) e -> p c e", p=128)))
        sX.dma(SY.dma_start(out=RBR[0:1, :], in_=E["rbt"][:]))
        sX.dma(SY.dma_start(out=SELE.rearrange("p a b -> p (a b)"), in_=E["sele"][:]))
        TE.wait_ge(sX.h, sX.v)
        sT.inc(TE.matmul(psA[:, 0, 0:64], ones32[:], RBR[0:1, :],
                         start=True, stop=True))
        VE.wait_ge(sT.h, sT.v)
        sV.inc(VE.tensor_copy(RBR[:, :], psA[:, 0, 0:64]))

        for tp in range(NTP):
            wfBase = sWF.v
            arrival = {}
            for e in range(NE + 1):
                for hc in range(HC):
                    for (WT, src, pos, ring, sems, kind) in (
                        (WS1, E["W1"][e] if e < NE else E["Ws1"], w1pos(e, hc), 4,
                         E["sW1"], "w1"),
                        (WS3, E["W3"][e] if e < NE else E["Ws3"], w3pos(e, hc), 4,
                         E["sW3"], "w3"),
                    ):
                        prev_hc = hc - ring
                        if prev_hc >= 0:
                            GP.wait_ge(sWF.h, wfBase + pos - 2 * ring + 1)
                        elif e > 0:
                            prev_pos = (e - 1) * 40 + 2 * (HC + prev_hc) + (pos % 2)
                            GP.wait_ge(sWF.h, wfBase + prev_pos + 1)
                        elif tp > 0:
                            GP.wait_ge(sWF.h, wfBase)
                        sl = sems[hc % 4]
                        sl.dma(GP.dma_start(
                            out=WT[:, hc % 4, :, :],
                            in_=src.rearrange("(c p) h -> p c h", p=128)
                                  [:, :, hc * 128:(hc + 1) * 128]))
                        arrival[(kind, e, hc)] = (sl, sl.v)
                for dcc in range(DC):
                    pos = w2pos(e, dcc)
                    prev_dcc = dcc - 2
                    if prev_dcc >= 0:
                        GP.wait_ge(sWF.h, wfBase + pos - 2 + 1)
                    elif e > 0:
                        prev_pos = (e - 1) * 40 + 32 + (DC + prev_dcc)
                        GP.wait_ge(sWF.h, wfBase + prev_pos + 1)
                    elif tp > 0:
                        GP.wait_ge(sWF.h, wfBase)
                    sl = E["sW2"][dcc % 2]
                    sl.dma(GP.dma_start(
                        out=WS2[:, dcc % 2, :, :],
                        in_=(E["W2"][e] if e < NE else E["Ws2"])
                            .rearrange("(c p) d -> p c d", p=128)
                            [:, :, dcc * 128:(dcc + 1) * 128]))
                    arrival[("w2", e, dcc)] = (sl, sl.v)

            SY.wait_ge(sSt.h, sSt.v)
            SY.wait_ge(sV.h, sV.v)
            sX.dma(SY.dma_start(
                out=YR[:],
                in_=y_scr.rearrange("(q s p) d -> q p s d", s=8, p=128)[tp]))
            VE.wait_ge(sX.h, sX.v)
            ln_stats(YR, 8)
            VE.wait_ge(sA.h, sA.v)
            rEv = sV.v
            for j in range(8):
                par = j % 2
                inst = VE.tensor_scalar(XL32[:], YR[:, j, :], MV[:, j, 0:1],
                                        RSTD[:, j:j + 1], ALU.subtract, ALU.mult)
                sV.inc(inst)
                TE.wait_ge(sV.h, sV.v)
                if j >= 2:
                    TE.wait_ge(sV.h, rEv)
                psTT = psA[:, par, :].rearrange("p (c d) -> p c d", d=128)
                for dc in range(DC):
                    inst = TE.transpose(psTT[:, dc, :],
                                        XL32[:, dc * 128:(dc + 1) * 128], identf[:])
                sT.inc(inst)
                VE.wait_ge(sT.h, sT.v)
                VE.tensor_scalar(XL2[:, :, j * 128:(j + 1) * 128], psTT[:, :, :],
                                 0.0, None, ALU.add)
                inst = VE.tensor_copy(X32R[:, par, :, :], psTT[:, :, :])
                rEv = sV.inc(inst)
                TE.wait_ge(sV.h, rEv)
                for dc in range(DC):
                    inst = TE.matmul(psB[:, par, 0:8], X32R[:, par, dc, :],
                                     Wr32[:, dc, :],
                                     start=(dc == 0), stop=(dc == DC - 1))
                sT.inc(inst)
                VE.wait_ge(sT.h, sT.v)
                inst = VE.tensor_scalar(LG[:, j, :], psB[:, par, 0:8], SCALE,
                                        None, ALU.mult)
                sV.inc(inst)
            VE.drain()
            VE.tensor_tensor(out=A8.rearrange("p a b -> p (a b)"),
                             in0=LG.rearrange("p a b -> p (a b)"), in1=RBR[:],
                             op=ALU.add)
            VE.drain()
            for j in range(8):
                VE.reduce_max(M12[:, j, 0:1], A8[:, j, :], axis=AX.X)
            VE.drain()
            for j in range(8):
                VE.tensor_scalar(MK1[:, j, :], A8[:, j, :], M12[:, j, 0:1],
                                 None, ALU.is_ge)
            VE.drain()
            VE.tensor_tensor(out=T8.rearrange("p a b -> p (a b)"),
                             in0=LG.rearrange("p a b -> p (a b)"),
                             in1=MK1.rearrange("p a b -> p (a b)"), op=ALU.mult)
            VE.drain()
            for j in range(8):
                VE.reduce_sum(L12[:, j, 0:1], T8[:, j, :], axis=AX.X)
            VE.scalar_tensor_tensor(out=A8.rearrange("p a b -> p (a b)"),
                                    in0=MK1.rearrange("p a b -> p (a b)"),
                                    scalar=-1e30,
                                    in1=A8.rearrange("p a b -> p (a b)"),
                                    op0=ALU.mult, op1=ALU.add)
            VE.drain()
            for j in range(8):
                VE.reduce_max(M12[:, j, 1:2], A8[:, j, :], axis=AX.X)
            VE.drain()
            for j in range(8):
                VE.tensor_scalar(MK2[:, j, :], A8[:, j, :], M12[:, j, 1:2],
                                 None, ALU.is_ge)
            VE.drain()
            VE.tensor_tensor(out=T8.rearrange("p a b -> p (a b)"),
                             in0=LG.rearrange("p a b -> p (a b)"),
                             in1=MK2.rearrange("p a b -> p (a b)"), op=ALU.mult)
            VE.drain()
            for j in range(8):
                VE.reduce_sum(L12[:, j, 1:2], T8[:, j, :], axis=AX.X)
            VE.drain()
            VE.tensor_tensor(out=NEGM[:], in0=L12[:, :, 0], in1=L12[:, :, 1],
                             op=ALU.max)
            VE.drain()
            VE.tensor_scalar(NEGM[:], NEGM[:], -1.0, None, ALU.mult)
            sV.inc(VE.drain())
            AC.wait_ge(sV.h, sV.v)
            for j in range(8):
                AC.activation(E12[:, j, 0:1], L12[:, j, 0:1], AF.Exp,
                              bias=NEGM[:, j:j + 1], scale=1.0)
                inst = AC.activation(E12[:, j, 1:2], L12[:, j, 1:2], AF.Exp,
                                     bias=NEGM[:, j:j + 1], scale=1.0)
            sA.inc(inst)
            VE.wait_ge(sA.h, sA.v)
            VE.tensor_tensor(out=ZZ[:], in0=E12[:, :, 0], in1=E12[:, :, 1], op=ALU.add)
            VE.drain()
            VE.reciprocal(ZI[:], ZZ[:])
            VE.drain()
            VE.tensor_tensor(out=WC[:, :, 0], in0=E12[:, :, 0], in1=ZI[:], op=ALU.mult)
            VE.tensor_tensor(out=WC[:, :, 1], in0=E12[:, :, 1], in1=ZI[:], op=ALU.mult)
            VE.drain()
            for j in range(8):
                VE.tensor_scalar(T8[:, j, :], MK1[:, j, :], WC[:, j, 0:1],
                                 None, ALU.mult)
            VE.drain()
            for j in range(8):
                VE.scalar_tensor_tensor(out=WFL[:, j, :], in0=MK2[:, j, :],
                                        scalar=WC[:, j, 1:2], in1=T8[:, j, :],
                                        op0=ALU.mult, op1=ALU.add)
            sV.inc(VE.drain())
            TE.wait_ge(sV.h, sV.v)
            for j in range(8):
                inst = TE.transpose(psB[0:8, 0, j * 128:(j + 1) * 128],
                                    WFL[:, j, :], identf[:])
            sT.inc(inst)
            VE.wait_ge(sT.h, sT.v)
            sV.inc(VE.tensor_scalar(WFT[:], psB[0:8, 0, :], 0.0, None, ALU.add))
            TE.wait_ge(sV.h, sV.v)
            TE.wait_ge(sX.h, sX.v)
            repEv = sV.v
            for e in range(NE):
                for half in range(2):
                    if e > 0 or half > 0:
                        TE.wait_ge(sV.h, repEv)
                    sT.inc(TE.matmul(psA[:, 0, half * 512:(half + 1) * 512],
                                     SELE[:, e, :],
                                     WFT[:, half * 512:(half + 1) * 512],
                                     start=True, stop=True))
                    VE.wait_ge(sT.h, sT.v)
                    inst = VE.tensor_scalar(WREP[:, e, half * 512:(half + 1) * 512],
                                            psA[:, 0, half * 512:(half + 1) * 512],
                                            0.0, None, ALU.add)
                    repEv = sV.inc(inst)

            for e in range(NE + 1):
                eV = sV.v
                eA = sA.v
                hEv_base = sV.v
                silA_base = sA.v
                for hc in range(HC):
                    par = hc % 2
                    _s, _v = arrival[("w1", e, hc)]
                    TE.wait_ge(_s.h, _v)
                    if hc >= 2:
                        TE.wait_ge(sV.h, hEv_base + 2 * (hc - 2) + 1)
                    else:
                        TE.wait_ge(sV.h, eV)
                    for u in range(2):
                        for dc in range(DC):
                            inst = TE.matmul(psA[:, par, u * 512:(u + 1) * 512],
                                             WS1[:, hc % 4, dc, :],
                                             XL2[:, dc, u * 512:(u + 1) * 512],
                                             start=(dc == 0), stop=(dc == DC - 1))
                    sWF.inc(inst)
                    aReady = sWF.v
                    _s, _v = arrival[("w3", e, hc)]
                    TE.wait_ge(_s.h, _v)
                    if hc >= 2:
                        TE.wait_ge(sA.h, silA_base + hc - 1)
                    elif e > 0 or tp > 0:
                        TE.wait_ge(sA.h, eA)
                    for u in range(2):
                        for dc in range(DC):
                            inst = TE.matmul(psB[:, par, u * 512:(u + 1) * 512],
                                             WS3[:, hc % 4, dc, :],
                                             XL2[:, dc, u * 512:(u + 1) * 512],
                                             start=(dc == 0), stop=(dc == DC - 1))
                    sWF.inc(inst)
                    AC.wait_ge(sWF.h, sWF.v)
                    inst = AC.activation(SILB[:, par, :], psB[:, par, :], AF.Silu,
                                         bias=0.0, scale=1.0)
                    sA.inc(inst)
                    VE.wait_ge(sWF.h, aReady)
                    VE.wait_ge(sA.h, sA.v)
                    inst = VE.tensor_tensor(out=HTMP[:, par, :], in0=psA[:, par, :],
                                            in1=SILB[:, par, :], op=ALU.mult)
                    sV.inc(inst)
                    VE.wait_ge(sV.h, sV.v)
                    if e < NE:
                        inst = VE.tensor_tensor(out=HW[:, hc, :], in0=HTMP[:, par, :],
                                                in1=WREP[:, e, :], op=ALU.mult)
                    else:
                        inst = VE.tensor_copy(HW[:, hc, :], HTMP[:, par, :])
                    sV.inc(inst)
                cEv = sV.v
                for dcc in range(DC):
                    par = dcc % 2
                    _s, _v = arrival[("w2", e, dcc)]
                    TE.wait_ge(_s.h, _v)
                    TE.wait_ge(sV.h, hEv_base + 2 * HC)
                    if dcc >= 2:
                        TE.wait_ge(sV.h, cEv + dcc - 1)
                    for u in range(2):
                        for hcc in range(HC):
                            inst = TE.matmul(psA[:, par, u * 512:(u + 1) * 512],
                                             WS2[:, dcc % 2, hcc, :],
                                             HW[:, hcc, u * 512:(u + 1) * 512],
                                             start=(hcc == 0), stop=(hcc == HC - 1))
                    sWF.inc(inst)
                    VE.wait_ge(sWF.h, sWF.v)
                    if e == 0:
                        inst = VE.tensor_scalar(OACC[:, dcc, :], psA[:, par, :], 0.0,
                                                None, ALU.add)
                    else:
                        inst = VE.scalar_tensor_tensor(out=OACC[:, dcc, :],
                                                       in0=psA[:, par, :], scalar=0.0,
                                                       in1=OACC[:, dcc, :],
                                                       op0=ALU.add, op1=ALU.add)
                    sV.inc(inst)

            if E["dbg2"] and tp == NTP - 1:
                AC.wait_ge(sV.h, sV.v)
                sSt.dma(AC.dma_start(out=E["dXL"][:],
                                     in_=XL2.rearrange("p a b -> p (a b)")))
                sSt.dma(AC.dma_start(out=E["dWF"][:],
                                     in_=WFL.rearrange("p a b -> p (a b)")))
                sSt.dma(AC.dma_start(out=E["dOA"][:],
                                     in_=OACC.rearrange("p a b -> p (a b)")))
                sSt.dma(AC.dma_start(out=E["dMV"][:],
                                     in_=E["MV"].rearrange("p a b -> p (a b)")))
            stBase = sSt.v
            TE.wait_ge(sA.h, sA.v)
            for j in range(8):
                par = j % 2
                TE.wait_ge(sV.h, sV.v)
                ps = psA if par == 0 else psB
                psTT = ps[:, 0, :].rearrange("p (c d) -> p c d", d=128)
                for dc in range(DC):
                    inst = TE.transpose(psTT[:, dc, :],
                                        OACC[:, dc, j * 128:(j + 1) * 128], identf[:])
                sT.inc(inst)
                VE.wait_ge(sT.h, sT.v)
                if j >= 2:
                    VE.wait_ge(sSt.h, sSt.v)
                inst = VE.tensor_tensor(out=OTM[:, par, :], in0=ps[:, 0, :],
                                        in1=YR[:, j, :], op=ALU.add)
                sV.inc(inst)
                AC.wait_ge(sV.h, sV.v)
                sSt.dma(AC.dma_start(
                    out=out_d.rearrange("(r p) d -> r p d", p=128)[tp * 8 + j],
                    in_=OTM[:, par, :]))


def kernel(L0_attn_x, L0_attn_w_qkv, L0_attn_b_qkv, L0_attn_w_o, L0_attn_b_o,
           L0_ffn_Wr, L0_ffn_rb, L0_ffn_W1, L0_ffn_W2, L0_ffn_W3,
           L0_ffn_Ws1, L0_ffn_Ws2, L0_ffn_Ws3):
    x = np.ascontiguousarray(np.asarray(L0_attn_x, dtype=np.float32))
    w_qkv = np.asarray(L0_attn_w_qkv, dtype=np.float32)
    b_qkv = np.asarray(L0_attn_b_qkv, dtype=np.float32)
    w_o = np.asarray(L0_attn_w_o, dtype=np.float32)
    b_o = np.asarray(L0_attn_b_o, dtype=np.float32)
    wT = np.ascontiguousarray(w_qkv.T)
    arrs = dict(
        wqT=np.ascontiguousarray(wT[:, 0:1024]),
        wkT=np.ascontiguousarray(wT[:, 1024:2048]),
        wvT=np.ascontiguousarray(wT[:, 2048:3072]),
        woT=np.ascontiguousarray(w_o.T),
        bq=np.ascontiguousarray(b_qkv[0:1024].reshape(8, 128).T),
        bk=np.ascontiguousarray(b_qkv[1024:2048].reshape(8, 128).T),
        bv=np.ascontiguousarray(b_qkv[2048:3072].reshape(1, 1024)),
        bo=np.ascontiguousarray(b_o.reshape(8, 128).T),
        Wr=np.ascontiguousarray(np.asarray(L0_ffn_Wr, dtype=np.float32)),
        rbt=np.ascontiguousarray(
            np.tile(np.asarray(L0_ffn_rb, dtype=np.float32).reshape(1, 8), (1, 8))),
        W1=np.ascontiguousarray(np.asarray(L0_ffn_W1, dtype=np.float32)),
        W2=np.ascontiguousarray(np.asarray(L0_ffn_W2, dtype=np.float32)),
        W3=np.ascontiguousarray(np.asarray(L0_ffn_W3, dtype=np.float32)),
        Ws1=np.ascontiguousarray(np.asarray(L0_ffn_Ws1, dtype=np.float32)),
        Ws2=np.ascontiguousarray(np.asarray(L0_ffn_Ws2, dtype=np.float32)),
        Ws3=np.ascontiguousarray(np.asarray(L0_ffn_Ws3, dtype=np.float32)),
        identb=np.eye(128, dtype=ml_dtypes.bfloat16),
        identf=np.eye(128, dtype=np.float32),
        sele=np.repeat(np.eye(8), 128, axis=1).astype(ml_dtypes.bfloat16),
        hmask=np.repeat(np.eye(2), 64, axis=0).astype(np.float32),
    )
    nc = build_nc()
    in_maps = []
    for c in range(8):
        b, hf = c // 2, c % 2
        m = dict(arrs)
        m["x_kv"] = np.ascontiguousarray(x[b])
        m["x_own"] = np.ascontiguousarray(x[b, hf * 2048:(hf + 1) * 2048])
        in_maps.append(m)
    res = run_bass_kernel_spmd(nc, in_maps, core_ids=list(range(8)))
    out = np.empty((4, 4096, 1024), dtype=np.float32)
    for c in range(8):
        b, hf = c // 2, c % 2
        out[b, hf * 2048:(hf + 1) * 2048] = res.results[c]["out"]
    return out


# version 09 (final)
# speedup vs baseline: 1.1121x; 1.1016x over previous
import numpy as np
import ml_dtypes
from contextlib import ExitStack
import concourse.bass as bass
import concourse.mybir as mybir
from concourse.alu_op_type import AluOpType as ALU
from concourse.bass_utils import run_bass_kernel_spmd

F32 = mybir.dt.float32
BF16 = mybir.dt.bfloat16
AF = mybir.ActivationFunctionType
AX = mybir.AxisListType

D = 1024
DC = 8
NP = 8
MKV = 4096
TOWN = 2048
QT = 512
NQT = TOWN // QT
NCH = MKV // 128
HID = 2048
HC = 16
NE = 8
TP = 1024
NTP = TOWN // TP
SCALE = 0.03125
EPS = 1e-5

DBG_NSUPER = 8
DBG_NTILE = NQT
DBG_NPAIR = NP
DBG_1A = 9


class Sem:
    def __init__(self, nc, name):
        self.h = nc.alloc_semaphore(name)
        self.v = 0

    def inc(self, inst, n=1):
        inst.then_inc(self.h, n)
        self.v += n
        return self.v

    def dma(self, inst):
        inst.then_inc(self.h, 16)
        self.v += 16
        return self.v


def build_nc(y_external=False, phases=(1, 2), y_input=False, dbg2=False):
    nc = bass.Bass("TRN2", target_bir_lowering=False)
    E = {}

    for nm, sh in [("x_kv", [MKV, D]), ("x_own", [TOWN, D]),
                   ("wqT", [D, D]), ("wkT", [D, D]), ("wvT", [D, D]), ("woT", [D, D]),
                   ("bq", [128, NP]), ("bk", [128, NP]), ("bv", [1, D]), ("bo", [128, DC]),
                   ("Wr", [D, NE]), ("rbt", [1, 64]),
                   ("W1", [NE, D, HID]), ("W2", [NE, HID, D]), ("W3", [NE, D, HID]),
                   ("Ws1", [D, HID]), ("Ws2", [HID, D]), ("Ws3", [D, HID]),
                   ("identf", [128, 128])]:
        E[nm] = nc.dram_tensor(nm, sh, F32, kind="ExternalInput")
    E["identb"] = nc.dram_tensor("identb", [128, 128], BF16, kind="ExternalInput")
    E["sele"] = nc.dram_tensor("sele", [8, NE * 128], BF16, kind="ExternalInput")
    E["hmask"] = nc.dram_tensor("hmask", [128, 2], F32, kind="ExternalInput")
    E["out_d"] = nc.dram_tensor("out", [TOWN, D], F32, kind="ExternalOutput")
    E["y_scr"] = nc.dram_tensor("y_scr", [TOWN, D], F32,
                                kind="ExternalInput" if y_input else
                                ("ExternalOutput" if y_external else "Internal"))
    E["dbg2"] = dbg2
    if dbg2:
        E["dXL"] = nc.dram_tensor("dXL", [128, DC * 1024], BF16, kind="ExternalOutput")
        E["dWF"] = nc.dram_tensor("dWF", [128, 64], F32, kind="ExternalOutput")
        E["dOA"] = nc.dram_tensor("dOA", [128, DC * 1024], F32, kind="ExternalOutput")
        E["dMV"] = nc.dram_tensor("dMV", [128, 16], F32, kind="ExternalOutput")

    E["identb_s"] = nc.alloc_sbuf_tensor("identb_s", [128, 128], BF16)
    E["identf_s"] = nc.alloc_sbuf_tensor("identf_s", [128, 128], F32)
    E["ones32"] = nc.alloc_sbuf_tensor("ones32", [1, 128], F32)
    E["bq_s"] = nc.alloc_sbuf_tensor("bq_s", [128, NP], F32)
    E["bk_s"] = nc.alloc_sbuf_tensor("bk_s", [128, NP], F32)
    E["bo_s"] = nc.alloc_sbuf_tensor("bo_s", [128, DC], F32)
    E["bvrow"] = nc.alloc_sbuf_tensor("bvrow", [1, D], BF16)
    E["KA8"] = nc.alloc_sbuf_tensor("KA8", [128, NP], F32)
    E["hmask_s"] = nc.alloc_sbuf_tensor("hmask_s", [128, 2], F32)
    E["ST6"] = nc.alloc_sbuf_tensor("ST6", [128, 8, 12], F32)
    E["MV"] = nc.alloc_sbuf_tensor("MV", [128, 8, 2], F32)
    E["LNV"] = nc.alloc_sbuf_tensor("LNV", [128, 8], F32)
    E["EPSV"] = nc.alloc_sbuf_tensor("EPSV", [128, 1], F32)
    E["onesb16"] = nc.alloc_sbuf_tensor("onesb16", [1, 128], BF16)
    E["RSTD"] = nc.alloc_sbuf_tensor("RSTD", [128, 8], F32)

    for s in ("sX", "sSt", "sG", "sT", "sV", "sA", "sWF"):
        E[s] = Sem(nc, s)
    E["sW1"] = [Sem(nc, f"sW1_{i}") for i in range(4)]
    E["sW3"] = [Sem(nc, f"sW3_{i}") for i in range(4)]
    E["sW2"] = [Sem(nc, f"sW2_{i}") for i in range(2)]
    sX, sSt, sG, sT, sV, sA = (E[k] for k in ("sX", "sSt", "sG", "sT", "sV", "sA"))
    TE, VE, AC, GP, SY = nc.tensor, nc.vector, nc.scalar, nc.gpsimd, nc.sync
    E["GPQ"] = GP

    def ln_stats(xt_ap, n_tsub):
        ST6, MV, LNV, RSTD = E["ST6"], E["MV"], E["LNV"], E["RSTD"]
        for j in range(n_tsub):
            VE.bn_stats(ST6[:, j, 0:6], xt_ap[:, j, 0:512])
            VE.bn_stats(ST6[:, j, 6:12], xt_ap[:, j, 512:1024])
        VE.drain()
        for j in range(n_tsub):
            VE.bn_aggr(MV[:, j, :], ST6[:, j, :])
        sV.inc(VE.drain())
        AC.wait_ge(sV.h, sV.v)
        AC.activation(LNV[:, 0:n_tsub], MV[:, 0:n_tsub, 1], AF.Ln,
                      bias=E["EPSV"][:], scale=1.0)
        AC.drain()
        inst = AC.activation(RSTD[:, 0:n_tsub], LNV[:, 0:n_tsub], AF.Exp,
                             bias=0.0, scale=-0.5)
        sA.inc(inst)

    E["ln_stats"] = ln_stats

    with nc.Block():
        for dst, src in [(E["identb_s"], E["identb"]), (E["identf_s"], E["identf"]),
                         (E["bq_s"], E["bq"]), (E["bk_s"], E["bk"]),
                         (E["bo_s"], E["bo"]), (E["hmask_s"], E["hmask"])]:
            sX.dma(SY.dma_start(out=dst[:], in_=src[:]))
        sG.dma(GP.dma_start(out=E["bvrow"][:], in_=E["bv"][:]))
        VE.wait_ge(sX.h, sX.v)
        VE.memset(E["ones32"][:], 1.0)
        VE.memset(E["onesb16"][:], 1.0)
        sV.inc(VE.memset(E["EPSV"][:], EPS))

        if 1 in phases:
            emit_phase1(nc, E)
        nc.all_engine_barrier()
        if 2 in phases:
            emit_phase2(nc, E)
        AC.wait_ge(sSt.h, sSt.v)
    return nc


def emit_phase1(nc, E):
    TE, VE, AC, GP, SY = nc.tensor, nc.vector, nc.scalar, nc.gpsimd, nc.sync
    sX, sSt, sG, sT, sV, sA = (E[k] for k in ("sX", "sSt", "sG", "sT", "sV", "sA"))
    x_kv, x_own, y_scr = E["x_kv"], E["x_own"], E["y_scr"]
    bq, bk, bo = E["bq_s"], E["bk_s"], E["bo_s"]
    identb, identf, ones32 = E["identb_s"], E["identf_s"], E["ones32"]
    KA8 = E["KA8"]
    MV, RSTD = E["MV"], E["RSTD"]
    ln_stats = E["ln_stats"]

    with ExitStack() as _esp1:
        _e0 = _esp1.enter_context
        KF = _e0(nc.sbuf_tensor("KF", [128, NP, MKV], BF16))
        VA = _e0(nc.sbuf_tensor("VA", [128, NCH, 1040], BF16))
        XT = _e0(nc.sbuf_tensor("XT", [128, 4, 1024], F32))
        XLNB = _e0(nc.sbuf_tensor("XLNB", [128, 1024], BF16))
        with ExitStack() as _es1a:
            _e = _es1a.enter_context
            WA = _e(nc.sbuf_tensor("WA", [128, DC, 1024], BF16))
            WB = _e(nc.sbuf_tensor("WB", [128, DC, 1024], BF16))
            XLNF = _e(nc.sbuf_tensor("XLNF", [128, DC, 512], BF16))
            bvr = _e(nc.sbuf_tensor("bvr", [128, D], F32))
            psTb = _e(nc.psum_tensor("psTb", [128, DC, 128], BF16))
            psK = _e(nc.psum_tensor("psK", [128, 2, 512], F32))
            psV = _e(nc.psum_tensor("psV", [128, 2, 512], F32))
            psM = _e(nc.psum_tensor("psM", [128, 512], F32))
            sG.dma(GP.dma_start(out=WA[:],
                                in_=E["wkT"].rearrange("(c p) e -> p c e", p=128)))
            w_arr = sG.dma(GP.dma_start(
                out=WB[:], in_=E["wvT"].rearrange("(c p) e -> p c e", p=128)))
            VE.wait_ge(sX.h, sX.v)
            if DBG_1A < 1:
                return
            for h in range(16):
                inst = VE.memset(VA[:, :, 65 * h + 64: 65 * h + 65], 1.0)
            sV.inc(inst)
            TE.wait_ge(sV.h, sV.v)
            TE.wait_ge(sG.h, sG.v)
            for half in range(2):
                if half > 0:
                    TE.wait_ge(sV.h, sV.v)
                sT.inc(TE.matmul(psM[:], E["onesb16"][:],
                                 E["bvrow"][:, half * 512:(half + 1) * 512],
                                 start=True, stop=True))
                VE.wait_ge(sT.h, sT.v)
                sV.inc(VE.tensor_copy(bvr[:, half * 512:(half + 1) * 512], psM[:]))

            xfree = sV.v
            for s in range(DBG_NSUPER):
                SY.wait_ge(sV.h, xfree)
                sX.dma(SY.dma_start(
                    out=XT[:],
                    in_=x_kv.rearrange("(s t p) d -> s p t d", t=4, p=128)[s]))
                VE.wait_ge(sX.h, sX.v)
                if DBG_1A < 2:
                    continue
                ln_stats(XT, 4)
                VE.wait_ge(sA.h, sA.v)
                for j in range(4):
                    inst = VE.tensor_scalar(XLNB[:], XT[:, j, :], MV[:, j, 0:1],
                                            RSTD[:, j:j + 1], ALU.subtract, ALU.mult)
                    sV.inc(inst)
                    if j == 3:
                        xfree = sV.v
                    if DBG_1A < 3:
                        continue
                    TE.wait_ge(sV.h, sV.v)
                    for dc in range(DC):
                        inst = TE.transpose(psTb[:, dc, :],
                                            XLNB[:, dc * 128:(dc + 1) * 128], identb[:])
                    sT.inc(inst)
                    VE.wait_ge(sT.h, sT.v)
                    inst = VE.tensor_copy(XLNF[:, :, j * 128:(j + 1) * 128], psTb[:])
                    sV.inc(inst)
                if DBG_1A < 4:
                    continue
                TE.wait_ge(sV.h, sV.v)
                if s == 0:
                    TE.wait_ge(sG.h, w_arr)
                evK = sV.v
                for p in range(NP):
                    if p >= 2:
                        TE.wait_ge(sV.h, evK + p - 1)
                    for dc in range(DC):
                        inst = TE.matmul(psK[:, p % 2, :],
                                         WA[:, dc, 128 * p:128 * (p + 1)],
                                         XLNF[:, dc, :],
                                         start=(dc == 0), stop=(dc == DC - 1))
                    sT.inc(inst)
                    VE.wait_ge(sT.h, sT.v)
                    inst = VE.tensor_scalar(KF[:, p, 512 * s:512 * (s + 1)],
                                            psK[:, p % 2, :], bk[:, p:p + 1],
                                            None, ALU.add)
                    sV.inc(inst)
                evV = sV.v
                if DBG_1A < 5:
                    continue
                for ms in range(4):
                    ch = 4 * s + ms
                    for half in range(2):
                        idx = ms * 2 + half
                        if idx >= 2:
                            TE.wait_ge(sV.h, evV + idx - 1)
                        for dc in range(DC):
                            inst = TE.matmul(psV[:, idx % 2, :],
                                             XLNF[:, dc, ms * 128:(ms + 1) * 128],
                                             WB[:, dc, half * 512:(half + 1) * 512],
                                             start=(dc == 0), stop=(dc == DC - 1))
                        sT.inc(inst)
                        VE.wait_ge(sT.h, sT.v)
                        inst = VE.tensor_tensor(
                            out=VA[:, ch, 520 * half:520 * half + 520]
                                .rearrange("p (h cc) -> p h cc", cc=65)[:, :, 0:64],
                            in0=psV[:, idx % 2, :].rearrange("p (h cc) -> p h cc", cc=64),
                            in1=bvr[:, half * 512:(half + 1) * 512]
                                .rearrange("p (h cc) -> p h cc", cc=64),
                            op=ALU.add)
                        sV.inc(inst)
            if DBG_1A < 6:
                return
            pass

        nc.all_engine_barrier()

        with ExitStack() as _es1b:
            _e = _es1b.enter_context
            Wsh = _e(nc.sbuf_tensor("Wsh", [128, DC, 1024], BF16))
            XLNF2 = _e(nc.sbuf_tensor("XLNF2", [128, DC, 512], BF16))
            QFm = _e(nc.sbuf_tensor("QFm", [128, NP, 512], BF16))
            PT = _e(nc.sbuf_tensor("PT", [128, 2, 2, 1024], BF16))
            HDS = _e(nc.sbuf_tensor("HDS", [128, NP, 512], BF16))
            tmpB = _e(nc.sbuf_tensor("tmpB", [64, 512], BF16))
            LROW = _e(nc.sbuf_tensor("LROW", [1, 2, 512], BF16))
            LINV = _e(nc.sbuf_tensor("LINV", [64, 512], BF16))
            PRJ = _e(nc.sbuf_tensor("PRJ", [128, 128], F32))
            YTM = _e(nc.sbuf_tensor("YTM", [128, 1024], F32))
            CMS = _e(nc.sbuf_tensor("CMS", [128, 64], BF16))
            CM1 = _e(nc.sbuf_tensor("CM1", [128, 1], F32))
            psS_A = _e(nc.psum_tensor("psS_A", [128, 2, 512], F32))
            psS_B = _e(nc.psum_tensor("psS_B", [128, 2, 512], F32))
            psPV_A = _e(nc.psum_tensor("psPV_A", [128, 512], F32))
            psPV_B = _e(nc.psum_tensor("psPV_B", [128, 512], F32))
            psM1 = _e(nc.psum_tensor("psM1", [128, 512], F32))
            psTb2 = _e(nc.psum_tensor("psTb2", [128, DC, 128], BF16))
            woDone = sT.v
            yDone = sSt.v
            for t in range(DBG_NTILE):
                if t > 0:
                    GP.wait_ge(sT.h, woDone)
                wq_arr = sG.dma(GP.dma_start(
                    out=Wsh[:], in_=E["wqT"].rearrange("(c p) e -> p c e", p=128)))
                if t > 0:
                    SY.wait_ge(sSt.h, yDone)
                sX.dma(SY.dma_start(
                    out=XT[:],
                    in_=x_own.rearrange("(s t p) d -> s p t d", t=4, p=128)[t]))
                VE.wait_ge(sX.h, sX.v)
                ln_stats(XT, 4)
                VE.wait_ge(sA.h, sA.v)
                for j in range(4):
                    inst = VE.tensor_scalar(XLNB[:], XT[:, j, :], MV[:, j, 0:1],
                                            RSTD[:, j:j + 1], ALU.subtract, ALU.mult)
                    sV.inc(inst)
                    TE.wait_ge(sV.h, sV.v)
                    for dc in range(DC):
                        inst = TE.transpose(psTb2[:, dc, :],
                                            XLNB[:, dc * 128:(dc + 1) * 128], identb[:])
                    sT.inc(inst)
                    VE.wait_ge(sT.h, sT.v)
                    inst = VE.tensor_copy(XLNF2[:, :, j * 128:(j + 1) * 128], psTb2[:])
                    sV.inc(inst)
                TE.wait_ge(sV.h, sV.v)
                TE.wait_ge(sG.h, wq_arr)
                evQ = sV.v
                for p in range(NP):
                    if p >= 1:
                        TE.wait_ge(sV.h, evQ + p)
                    for dc in range(DC):
                        inst = TE.matmul(psM1[:], Wsh[:, dc, 128 * p:128 * (p + 1)],
                                         XLNF2[:, dc, :],
                                         start=(dc == 0), stop=(dc == DC - 1))
                    sT.inc(inst)
                    qDone = sT.v
                    VE.wait_ge(sT.h, sT.v)
                    inst = VE.tensor_scalar(QFm[:, p, :], psM1[:], bq[:, p:p + 1],
                                            None, ALU.add)
                    sV.inc(inst)
                GP.wait_ge(sT.h, qDone)
                wo_arr = sG.dma(GP.dma_start(
                    out=Wsh[:], in_=E["woT"].rearrange("(c p) e -> p c e", p=128)))
                TE.wait_ge(sV.h, sV.v)
                smBase = sV.v
                si = 0
                for p in range(DBG_NPAIR):
                    for cj in (0, 8, 16, 24):
                        if si >= 2:
                            TE.wait_ge(sV.h, smBase + si - 1)
                        TE.matmul(psS_A[:, si % 2, :],
                                  KF[0:64, p, 128 * cj:128 * (cj + 1)],
                                  QFm[0:64, p, :], start=True, stop=True,
                                  tile_position=(0, 0))
                        inst = TE.matmul(psS_B[:, si % 2, :],
                                         KF[64:128, p, 128 * cj:128 * (cj + 1)],
                                         QFm[64:128, p, :], start=True, stop=True,
                                         tile_position=(64, 0))
                        sT.inc(inst)
                        VE.wait_ge(sT.h, sT.v)
                        sampA = psS_A[:, si % 2, :].rearrange(
                            "p (a b) -> p a b", b=4)[:, :, 0]
                        sampB = psS_B[:, si % 2, :].rearrange(
                            "p (a b) -> p a b", b=4)[:, :, 0]
                        with nc.allow_low_precision(
                                reason="sampled maxes for the softmax bound; the "
                                       "+20 margin absorbs bf16 rounding"):
                            VE.reduce_max(CMS[:, 2 * si:2 * si + 1], sampA, axis=AX.X)
                            inst = VE.reduce_max(CMS[:, 2 * si + 1:2 * si + 2], sampB,
                                                 axis=AX.X)
                        sV.inc(inst)
                        si += 1
                VE.drain()
                VE.reduce_max(CM1[:], CMS[:], axis=AX.X)
                sV.inc(VE.drain())
                TE.wait_ge(sV.h, sV.v)
                sT.inc(TE.transpose(psM1[0:1, 0:128], CM1[:], identf[:]))
                VE.wait_ge(sT.h, sT.v)
                VE.reduce_max(PRJ[0:1, 0:1], psM1[0:1, 0:128], axis=AX.X)
                VE.drain()
                VE.tensor_scalar(PRJ[0:1, 1:2], PRJ[0:1, 0:1], SCALE, 20.0,
                                 ALU.mult, ALU.add)
                VE.drain()
                VE.tensor_scalar(PRJ[0:1, 2:3], PRJ[0:1, 1:2], -1.0, None, ALU.mult)
                sV.inc(VE.drain())
                TE.wait_ge(sV.h, sV.v)
                sT.inc(TE.matmul(psM1[:, 0:1], ones32[:], PRJ[0:1, 2:3],
                                 start=True, stop=True))
                VE.wait_ge(sT.h, sT.v)
                negg_v = sV.inc(VE.tensor_copy(PRJ[:, 4:5], psM1[:, 0:1]))
                AC.wait_ge(sV.h, negg_v)
                TE.wait_ge(sV.h, negg_v)

                epiV = sV.v
                shiftSt = sSt.v
                LNT = YTM[0:1, :].rearrange("p (a b) -> p a b", b=512)
                for p in range(DBG_NPAIR):
                    tBase = sT.v
                    aBase = sA.v
                    for k in range(17):
                        if k < 16:
                            if k >= 1:
                                TE.wait_ge(sA.h, aBase + 2 * k - 1)
                            for u in range(2):
                                j = 2 * k + u
                                sT.inc(TE.matmul(
                                    psS_A[:, u, :], KF[0:64, p, 128 * j:128 * (j + 1)],
                                    QFm[0:64, p, :], start=True, stop=True,
                                    tile_position=(0, 0)))
                            if k >= 1:
                                TE.wait_ge(sA.h, aBase + 2 * k)
                            for u in range(2):
                                j = 2 * k + u
                                sT.inc(TE.matmul(
                                    psS_B[:, u, :], KF[64:128, p, 128 * j:128 * (j + 1)],
                                    QFm[64:128, p, :], start=True, stop=True,
                                    tile_position=(64, 0)))
                        if k >= 1:
                            kk = k - 1
                            TE.wait_ge(sA.h, aBase + 2 * kk + 1)
                            if p > 0 and kk == 0:
                                TE.wait_ge(sV.h, epiV)
                            for u in range(2):
                                j = 2 * kk + u
                                sT.inc(TE.matmul(
                                    psPV_A[0:65, :],
                                    VA[:, j, 65 * (2 * p):65 * (2 * p) + 65],
                                    PT[:, 0, kk % 2, u * 512:(u + 1) * 512],
                                    start=(j == 0), stop=(j == NCH - 1)))
                            TE.wait_ge(sA.h, aBase + 2 * kk + 2)
                            for u in range(2):
                                j = 2 * kk + u
                                sT.inc(TE.matmul(
                                    psPV_B[0:65, :],
                                    VA[:, j, 65 * (2 * p + 1):65 * (2 * p + 1) + 65],
                                    PT[:, 1, kk % 2, u * 512:(u + 1) * 512],
                                    start=(j == 0), stop=(j == NCH - 1)))
                        if k < 16:
                            AC.wait_ge(sT.h,
                                       tBase + (2 if k == 0 else 4 + 8 * (k - 1) + 2))
                            sA.inc(AC.activation(
                                PT[:, 0, k % 2, :],
                                psS_A.rearrange("p a b -> p (a b)"),
                                AF.Exp, bias=PRJ[:, 4:5], scale=SCALE))
                            AC.wait_ge(sT.h,
                                       tBase + (4 if k == 0 else 4 + 8 * (k - 1) + 4))
                            sA.inc(AC.activation(
                                PT[:, 1, k % 2, :],
                                psS_B.rearrange("p a b -> p (a b)"),
                                AF.Exp, bias=PRJ[:, 4:5], scale=SCALE))
                    AC.wait_ge(sT.h, sT.v)
                    AC.activation(LNT[:, 0, :], psPV_A[64:65, :], AF.Ln,
                                  bias=0.0, scale=1.0)
                    AC.activation(LNT[:, 1, :], psPV_B[64:65, :], AF.Ln,
                                  bias=0.0, scale=1.0)
                    AC.drain()
                    with nc.allow_low_precision(
                            reason="1/l stored bf16; heads are bf16 anyway"):
                        inst = AC.activation(LROW[:, :, :], LNT[:, :, :], AF.Exp,
                                             bias=0.0, scale=-1.0)
                    sA.inc(inst)
                    TE.wait_ge(sA.h, sA.v)
                    sT.inc(TE.matmul(psM1[0:64, :], E["onesb16"][0:1, 0:64],
                                     LROW[:, 0, :], start=True, stop=True))
                    VE.wait_ge(sT.h, sT.v)
                    VE.tensor_scalar(LINV[:], psM1[0:64, :], 0.0, None, ALU.add)
                    VE.drain()
                    inst = VE.tensor_tensor(out=HDS[0:64, p, :], in0=psPV_A[0:64, :],
                                            in1=LINV[:], op=ALU.mult)
                    sV.inc(inst)
                    TE.wait_ge(sV.h, sV.v)
                    sT.inc(TE.matmul(psM1[0:64, :], E["onesb16"][0:1, 0:64],
                                     LROW[:, 1, :], start=True, stop=True))
                    VE.wait_ge(sT.h, sT.v)
                    VE.tensor_scalar(LINV[:], psM1[0:64, :], 0.0, None, ALU.add)
                    VE.drain()
                    if p > 0:
                        VE.wait_ge(sSt.h, shiftSt + 16 * p)
                    inst = VE.tensor_tensor(out=tmpB[:], in0=psPV_B[0:64, :],
                                            in1=LINV[:], op=ALU.mult)
                    epiV = sV.inc(inst)
                    AC.wait_ge(sV.h, epiV)
                    sSt.dma(AC.dma_start(out=HDS[64:128, p, :], in_=tmpB[:]))
                TE.wait_ge(sSt.h, sSt.v)
                TE.wait_ge(sG.h, wo_arr)
                ytmV = sV.v
                for j in range(4):
                    psT = psS_A if j % 2 == 0 else psS_B
                    psTbig = psT.rearrange("p a b -> p (a b)")
                    psTv = psTbig.rearrange("p (c d) -> p c d", d=128)
                    if j >= 2:
                        TE.wait_ge(sV.h, ytmV)
                    evW = sV.v
                    for dc in range(DC):
                        wo_ps = psM1[:, 0:128] if dc % 2 == 0 else psPV_A[:, 0:128]
                        if dc >= 2:
                            TE.wait_ge(sV.h, evW + dc - 1)
                        for rc in range(DC):
                            inst = TE.matmul(wo_ps,
                                             Wsh[:, rc, 128 * dc:128 * (dc + 1)],
                                             HDS[:, rc, j * 128:(j + 1) * 128],
                                             start=(rc == 0), stop=(rc == DC - 1))
                        sT.inc(inst)
                        VE.wait_ge(sT.h, sT.v)
                        inst = VE.tensor_scalar(PRJ[:], wo_ps,
                                                bo[:, dc:dc + 1], None, ALU.add)
                        sV.inc(inst)
                        TE.wait_ge(sV.h, sV.v)
                        sT.inc(TE.transpose(psTv[:, dc, :], PRJ[:], identf[:]))
                    VE.wait_ge(sT.h, sT.v)
                    if j > 0:
                        VE.wait_ge(sSt.h, sSt.v)
                    VE.tensor_scalar(XLNB[:], XT[:, j, :], MV[:, j, 0:1],
                                     RSTD[:, j:j + 1], ALU.subtract, ALU.mult)
                    VE.tensor_tensor(out=YTM[:], in0=psTbig, in1=XT[:, j, :],
                                     op=ALU.add)
                    VE.drain()
                    inst = VE.tensor_tensor(out=YTM[:], in0=YTM[:], in1=XLNB[:],
                                            op=ALU.add)
                    ytmV = sV.inc(inst)
                    AC.wait_ge(sV.h, ytmV)
                    sSt.dma(AC.dma_start(
                        out=y_scr.rearrange("(r p) d -> r p d", p=128)[4 * t + j],
                        in_=YTM[:]))
                woDone = sT.v
                yDone = sSt.v
            SY.wait_ge(sSt.h, sSt.v)


def emit_phase2(nc, E):
    TE, VE, AC, GP, SY = nc.tensor, nc.vector, nc.scalar, nc.gpsimd, nc.sync
    sX, sSt, sG, sT, sV, sA, sWF = (E[k] for k in
                                    ("sX", "sSt", "sG", "sT", "sV", "sA", "sWF"))
    y_scr, out_d = E["y_scr"], E["out_d"]
    identf, ones32 = E["identf_s"], E["ones32"]
    MV, RSTD = E["MV"], E["RSTD"]
    ln_stats = E["ln_stats"]

    def w1pos(e, hc):
        return e * 40 + 2 * hc

    def w3pos(e, hc):
        return e * 40 + 2 * hc + 1

    def w2pos(e, dcc):
        return e * 40 + 32 + dcc

    NPOS = (NE + 1) * 40

    with ExitStack() as _es2:
        _e2 = _es2.enter_context
        YR = _e2(nc.sbuf_tensor("YR", [128, 8, 1024], F32))
        XL2 = _e2(nc.sbuf_tensor("XL2", [128, DC, 1024], BF16))
        XL32 = _e2(nc.sbuf_tensor("XL32", [128, 1024], F32))
        X32R = _e2(nc.sbuf_tensor("X32R", [128, 2, DC, 128], F32))
        HW = _e2(nc.sbuf_tensor("HW", [128, HC, 1024], BF16))
        OACC = _e2(nc.sbuf_tensor("OACC", [128, DC, 1024], F32))
        WS1 = _e2(nc.sbuf_tensor("WS1", [128, 4, DC, 128], BF16))
        WS3 = _e2(nc.sbuf_tensor("WS3", [128, 4, DC, 128], BF16))
        WS2 = _e2(nc.sbuf_tensor("WS2", [128, 2, HC, 128], BF16))
        WREP = _e2(nc.sbuf_tensor("WREP", [128, NE, 1024], BF16))
        SILB = _e2(nc.sbuf_tensor("SILB", [128, 2, 1024], BF16))
        HTMP = _e2(nc.sbuf_tensor("HTMP", [128, 2, 1024], BF16))
        Wr32 = _e2(nc.sbuf_tensor("Wr32", [128, DC, 8], F32))
        RBR = _e2(nc.sbuf_tensor("RBR", [128, 64], F32))
        LG = _e2(nc.sbuf_tensor("LG", [128, 8, 8], F32))
        A8 = _e2(nc.sbuf_tensor("A8", [128, 8, 8], F32))
        MK1 = _e2(nc.sbuf_tensor("MK1", [128, 8, 8], F32))
        MK2 = _e2(nc.sbuf_tensor("MK2", [128, 8, 8], F32))
        T8 = _e2(nc.sbuf_tensor("T8", [128, 8, 8], F32))
        WFL = _e2(nc.sbuf_tensor("WFL", [128, 8, 8], F32))
        M12 = _e2(nc.sbuf_tensor("M12", [128, 8, 2], F32))
        L12 = _e2(nc.sbuf_tensor("L12", [128, 8, 2], F32))
        E12 = _e2(nc.sbuf_tensor("E12", [128, 8, 2], F32))
        NEGM = _e2(nc.sbuf_tensor("NEGM", [128, 8], F32))
        ZZ = _e2(nc.sbuf_tensor("ZZ", [128, 8], F32))
        ZI = _e2(nc.sbuf_tensor("ZI", [128, 8], F32))
        WC = _e2(nc.sbuf_tensor("WC", [128, 8, 2], F32))
        WFT = _e2(nc.sbuf_tensor("WFT", [8, 1024], BF16))
        SELE = _e2(nc.sbuf_tensor("SELE", [8, NE, 128], BF16))
        OTM = _e2(nc.sbuf_tensor("OTM", [128, 2, 1024], F32))
        psA = _e2(nc.psum_tensor("psA", [128, 2, 1024], F32))
        psB = _e2(nc.psum_tensor("psB", [128, 2, 1024], F32))
        sX.dma(SY.dma_start(out=Wr32[:],
                            in_=E["Wr"].rearrange("(c p) e -> p c e", p=128)))
        sX.dma(SY.dma_start(out=RBR[0:1, :], in_=E["rbt"][:]))
        sX.dma(SY.dma_start(out=SELE.rearrange("p a b -> p (a b)"), in_=E["sele"][:]))
        TE.wait_ge(sX.h, sX.v)
        sT.inc(TE.matmul(psA[:, 0, 0:64], ones32[:], RBR[0:1, :],
                         start=True, stop=True))
        VE.wait_ge(sT.h, sT.v)
        sV.inc(VE.tensor_copy(RBR[:, :], psA[:, 0, 0:64]))

        for tp in range(NTP):
            wfBase = sWF.v
            arrival = {}
            for e in range(NE + 1):
                for hc in range(HC):
                    for (WT, src, pos, ring, sems, kind) in (
                        (WS1, E["W1"][e] if e < NE else E["Ws1"], w1pos(e, hc), 4,
                         E["sW1"], "w1"),
                        (WS3, E["W3"][e] if e < NE else E["Ws3"], w3pos(e, hc), 4,
                         E["sW3"], "w3"),
                    ):
                        prev_hc = hc - ring
                        if prev_hc >= 0:
                            GP.wait_ge(sWF.h, wfBase + pos - 2 * ring + 1)
                        elif e > 0:
                            prev_pos = (e - 1) * 40 + 2 * (HC + prev_hc) + (pos % 2)
                            GP.wait_ge(sWF.h, wfBase + prev_pos + 1)
                        elif tp > 0:
                            GP.wait_ge(sWF.h, wfBase)
                        sl = sems[hc % 4]
                        sl.dma(GP.dma_start(
                            out=WT[:, hc % 4, :, :],
                            in_=src.rearrange("(c p) h -> p c h", p=128)
                                  [:, :, hc * 128:(hc + 1) * 128]))
                        arrival[(kind, e, hc)] = (sl, sl.v)
                for dcc in range(DC):
                    pos = w2pos(e, dcc)
                    prev_dcc = dcc - 2
                    if prev_dcc >= 0:
                        GP.wait_ge(sWF.h, wfBase + pos - 2 + 1)
                    elif e > 0:
                        prev_pos = (e - 1) * 40 + 32 + (DC + prev_dcc)
                        GP.wait_ge(sWF.h, wfBase + prev_pos + 1)
                    elif tp > 0:
                        GP.wait_ge(sWF.h, wfBase)
                    sl = E["sW2"][dcc % 2]
                    sl.dma(GP.dma_start(
                        out=WS2[:, dcc % 2, :, :],
                        in_=(E["W2"][e] if e < NE else E["Ws2"])
                            .rearrange("(c p) d -> p c d", p=128)
                            [:, :, dcc * 128:(dcc + 1) * 128]))
                    arrival[("w2", e, dcc)] = (sl, sl.v)

            SY.wait_ge(sSt.h, sSt.v)
            SY.wait_ge(sV.h, sV.v)
            sX.dma(SY.dma_start(
                out=YR[:],
                in_=y_scr.rearrange("(q s p) d -> q p s d", s=8, p=128)[tp]))
            VE.wait_ge(sX.h, sX.v)
            ln_stats(YR, 8)
            VE.wait_ge(sA.h, sA.v)
            rEv = sV.v
            for j in range(8):
                par = j % 2
                inst = VE.tensor_scalar(XL32[:], YR[:, j, :], MV[:, j, 0:1],
                                        RSTD[:, j:j + 1], ALU.subtract, ALU.mult)
                sV.inc(inst)
                TE.wait_ge(sV.h, sV.v)
                if j >= 2:
                    TE.wait_ge(sV.h, rEv)
                psTT = psA[:, par, :].rearrange("p (c d) -> p c d", d=128)
                for dc in range(DC):
                    inst = TE.transpose(psTT[:, dc, :],
                                        XL32[:, dc * 128:(dc + 1) * 128], identf[:])
                sT.inc(inst)
                VE.wait_ge(sT.h, sT.v)
                VE.tensor_scalar(XL2[:, :, j * 128:(j + 1) * 128], psTT[:, :, :],
                                 0.0, None, ALU.add)
                inst = VE.tensor_copy(X32R[:, par, :, :], psTT[:, :, :])
                rEv = sV.inc(inst)
                TE.wait_ge(sV.h, rEv)
                for dc in range(DC):
                    inst = TE.matmul(psB[:, par, 0:8], X32R[:, par, dc, :],
                                     Wr32[:, dc, :],
                                     start=(dc == 0), stop=(dc == DC - 1))
                sT.inc(inst)
                VE.wait_ge(sT.h, sT.v)
                inst = VE.tensor_scalar(LG[:, j, :], psB[:, par, 0:8], SCALE,
                                        None, ALU.mult)
                sV.inc(inst)
            VE.drain()
            VE.tensor_tensor(out=A8.rearrange("p a b -> p (a b)"),
                             in0=LG.rearrange("p a b -> p (a b)"), in1=RBR[:],
                             op=ALU.add)
            VE.drain()
            for j in range(8):
                VE.reduce_max(M12[:, j, 0:1], A8[:, j, :], axis=AX.X)
            VE.drain()
            for j in range(8):
                VE.tensor_scalar(MK1[:, j, :], A8[:, j, :], M12[:, j, 0:1],
                                 None, ALU.is_ge)
            VE.drain()
            VE.tensor_tensor(out=T8.rearrange("p a b -> p (a b)"),
                             in0=LG.rearrange("p a b -> p (a b)"),
                             in1=MK1.rearrange("p a b -> p (a b)"), op=ALU.mult)
            VE.drain()
            for j in range(8):
                VE.reduce_sum(L12[:, j, 0:1], T8[:, j, :], axis=AX.X)
            VE.scalar_tensor_tensor(out=A8.rearrange("p a b -> p (a b)"),
                                    in0=MK1.rearrange("p a b -> p (a b)"),
                                    scalar=-1e30,
                                    in1=A8.rearrange("p a b -> p (a b)"),
                                    op0=ALU.mult, op1=ALU.add)
            VE.drain()
            for j in range(8):
                VE.reduce_max(M12[:, j, 1:2], A8[:, j, :], axis=AX.X)
            VE.drain()
            for j in range(8):
                VE.tensor_scalar(MK2[:, j, :], A8[:, j, :], M12[:, j, 1:2],
                                 None, ALU.is_ge)
            VE.drain()
            VE.tensor_tensor(out=T8.rearrange("p a b -> p (a b)"),
                             in0=LG.rearrange("p a b -> p (a b)"),
                             in1=MK2.rearrange("p a b -> p (a b)"), op=ALU.mult)
            VE.drain()
            for j in range(8):
                VE.reduce_sum(L12[:, j, 1:2], T8[:, j, :], axis=AX.X)
            VE.drain()
            VE.tensor_tensor(out=NEGM[:], in0=L12[:, :, 0], in1=L12[:, :, 1],
                             op=ALU.max)
            VE.drain()
            VE.tensor_scalar(NEGM[:], NEGM[:], -1.0, None, ALU.mult)
            sV.inc(VE.drain())
            AC.wait_ge(sV.h, sV.v)
            for j in range(8):
                AC.activation(E12[:, j, 0:1], L12[:, j, 0:1], AF.Exp,
                              bias=NEGM[:, j:j + 1], scale=1.0)
                inst = AC.activation(E12[:, j, 1:2], L12[:, j, 1:2], AF.Exp,
                                     bias=NEGM[:, j:j + 1], scale=1.0)
            sA.inc(inst)
            VE.wait_ge(sA.h, sA.v)
            VE.tensor_tensor(out=ZZ[:], in0=E12[:, :, 0], in1=E12[:, :, 1], op=ALU.add)
            VE.drain()
            VE.reciprocal(ZI[:], ZZ[:])
            VE.drain()
            VE.tensor_tensor(out=WC[:, :, 0], in0=E12[:, :, 0], in1=ZI[:], op=ALU.mult)
            VE.tensor_tensor(out=WC[:, :, 1], in0=E12[:, :, 1], in1=ZI[:], op=ALU.mult)
            VE.drain()
            for j in range(8):
                VE.tensor_scalar(T8[:, j, :], MK1[:, j, :], WC[:, j, 0:1],
                                 None, ALU.mult)
            VE.drain()
            for j in range(8):
                VE.scalar_tensor_tensor(out=WFL[:, j, :], in0=MK2[:, j, :],
                                        scalar=WC[:, j, 1:2], in1=T8[:, j, :],
                                        op0=ALU.mult, op1=ALU.add)
            sV.inc(VE.drain())
            TE.wait_ge(sV.h, sV.v)
            for j in range(8):
                inst = TE.transpose(psB[0:8, 0, j * 128:(j + 1) * 128],
                                    WFL[:, j, :], identf[:])
            sT.inc(inst)
            VE.wait_ge(sT.h, sT.v)
            sV.inc(VE.tensor_scalar(WFT[:], psB[0:8, 0, :], 0.0, None, ALU.add))
            TE.wait_ge(sV.h, sV.v)
            TE.wait_ge(sX.h, sX.v)
            repEv = sV.v
            for e in range(NE):
                for half in range(2):
                    if e > 0 or half > 0:
                        TE.wait_ge(sV.h, repEv)
                    sT.inc(TE.matmul(psA[:, 0, half * 512:(half + 1) * 512],
                                     SELE[:, e, :],
                                     WFT[:, half * 512:(half + 1) * 512],
                                     start=True, stop=True))
                    VE.wait_ge(sT.h, sT.v)
                    inst = VE.tensor_scalar(WREP[:, e, half * 512:(half + 1) * 512],
                                            psA[:, 0, half * 512:(half + 1) * 512],
                                            0.0, None, ALU.add)
                    repEv = sV.inc(inst)

            for e in range(NE + 1):
                eV = sV.v
                eA = sA.v
                hEv_base = sV.v
                silA_base = sA.v
                for hc in range(HC):
                    par = hc % 2
                    _s, _v = arrival[("w1", e, hc)]
                    TE.wait_ge(_s.h, _v)
                    if hc >= 2:
                        TE.wait_ge(sV.h, hEv_base + 2 * (hc - 2) + 1)
                    else:
                        TE.wait_ge(sV.h, eV)
                    for u in range(2):
                        for dc in range(DC):
                            inst = TE.matmul(psA[:, par, u * 512:(u + 1) * 512],
                                             WS1[:, hc % 4, dc, :],
                                             XL2[:, dc, u * 512:(u + 1) * 512],
                                             start=(dc == 0), stop=(dc == DC - 1))
                    sWF.inc(inst)
                    aReady = sWF.v
                    _s, _v = arrival[("w3", e, hc)]
                    TE.wait_ge(_s.h, _v)
                    if hc >= 2:
                        TE.wait_ge(sA.h, silA_base + hc - 1)
                    elif e > 0 or tp > 0:
                        TE.wait_ge(sA.h, eA)
                    for u in range(2):
                        for dc in range(DC):
                            inst = TE.matmul(psB[:, par, u * 512:(u + 1) * 512],
                                             WS3[:, hc % 4, dc, :],
                                             XL2[:, dc, u * 512:(u + 1) * 512],
                                             start=(dc == 0), stop=(dc == DC - 1))
                    sWF.inc(inst)
                    AC.wait_ge(sWF.h, sWF.v)
                    inst = AC.activation(SILB[:, par, :], psB[:, par, :], AF.Silu,
                                         bias=0.0, scale=1.0)
                    sA.inc(inst)
                    VE.wait_ge(sWF.h, aReady)
                    VE.wait_ge(sA.h, sA.v)
                    inst = VE.tensor_tensor(out=HTMP[:, par, :], in0=psA[:, par, :],
                                            in1=SILB[:, par, :], op=ALU.mult)
                    sV.inc(inst)
                    VE.wait_ge(sV.h, sV.v)
                    if e < NE:
                        inst = VE.tensor_tensor(out=HW[:, hc, :], in0=HTMP[:, par, :],
                                                in1=WREP[:, e, :], op=ALU.mult)
                    else:
                        inst = VE.tensor_copy(HW[:, hc, :], HTMP[:, par, :])
                    sV.inc(inst)
                cEv = sV.v
                for dcc in range(DC):
                    par = dcc % 2
                    _s, _v = arrival[("w2", e, dcc)]
                    TE.wait_ge(_s.h, _v)
                    TE.wait_ge(sV.h, hEv_base + 2 * HC)
                    if dcc >= 2:
                        TE.wait_ge(sV.h, cEv + dcc - 1)
                    for u in range(2):
                        for hcc in range(HC):
                            inst = TE.matmul(psA[:, par, u * 512:(u + 1) * 512],
                                             WS2[:, dcc % 2, hcc, :],
                                             HW[:, hcc, u * 512:(u + 1) * 512],
                                             start=(hcc == 0), stop=(hcc == HC - 1))
                    sWF.inc(inst)
                    VE.wait_ge(sWF.h, sWF.v)
                    if e == 0:
                        inst = VE.tensor_scalar(OACC[:, dcc, :], psA[:, par, :], 0.0,
                                                None, ALU.add)
                    else:
                        inst = VE.scalar_tensor_tensor(out=OACC[:, dcc, :],
                                                       in0=psA[:, par, :], scalar=0.0,
                                                       in1=OACC[:, dcc, :],
                                                       op0=ALU.add, op1=ALU.add)
                    sV.inc(inst)

            if E["dbg2"] and tp == NTP - 1:
                AC.wait_ge(sV.h, sV.v)
                sSt.dma(AC.dma_start(out=E["dXL"][:],
                                     in_=XL2.rearrange("p a b -> p (a b)")))
                sSt.dma(AC.dma_start(out=E["dWF"][:],
                                     in_=WFL.rearrange("p a b -> p (a b)")))
                sSt.dma(AC.dma_start(out=E["dOA"][:],
                                     in_=OACC.rearrange("p a b -> p (a b)")))
                sSt.dma(AC.dma_start(out=E["dMV"][:],
                                     in_=E["MV"].rearrange("p a b -> p (a b)")))
            stBase = sSt.v
            TE.wait_ge(sA.h, sA.v)
            for j in range(8):
                par = j % 2
                TE.wait_ge(sV.h, sV.v)
                ps = psA if par == 0 else psB
                psTT = ps[:, 0, :].rearrange("p (c d) -> p c d", d=128)
                for dc in range(DC):
                    inst = TE.transpose(psTT[:, dc, :],
                                        OACC[:, dc, j * 128:(j + 1) * 128], identf[:])
                sT.inc(inst)
                VE.wait_ge(sT.h, sT.v)
                if j >= 2:
                    VE.wait_ge(sSt.h, sSt.v)
                inst = VE.tensor_tensor(out=OTM[:, par, :], in0=ps[:, 0, :],
                                        in1=YR[:, j, :], op=ALU.add)
                sV.inc(inst)
                AC.wait_ge(sV.h, sV.v)
                sSt.dma(AC.dma_start(
                    out=out_d.rearrange("(r p) d -> r p d", p=128)[tp * 8 + j],
                    in_=OTM[:, par, :]))


def kernel(L0_attn_x, L0_attn_w_qkv, L0_attn_b_qkv, L0_attn_w_o, L0_attn_b_o,
           L0_ffn_Wr, L0_ffn_rb, L0_ffn_W1, L0_ffn_W2, L0_ffn_W3,
           L0_ffn_Ws1, L0_ffn_Ws2, L0_ffn_Ws3):
    x = np.ascontiguousarray(np.asarray(L0_attn_x, dtype=np.float32))
    w_qkv = np.asarray(L0_attn_w_qkv, dtype=np.float32)
    b_qkv = np.asarray(L0_attn_b_qkv, dtype=np.float32)
    w_o = np.asarray(L0_attn_w_o, dtype=np.float32)
    b_o = np.asarray(L0_attn_b_o, dtype=np.float32)
    wT = np.ascontiguousarray(w_qkv.T)
    arrs = dict(
        wqT=np.ascontiguousarray(wT[:, 0:1024]),
        wkT=np.ascontiguousarray(wT[:, 1024:2048]),
        wvT=np.ascontiguousarray(wT[:, 2048:3072]),
        woT=np.ascontiguousarray(w_o.T),
        bq=np.ascontiguousarray(b_qkv[0:1024].reshape(8, 128).T),
        bk=np.ascontiguousarray(b_qkv[1024:2048].reshape(8, 128).T),
        bv=np.ascontiguousarray(b_qkv[2048:3072].reshape(1, 1024)),
        bo=np.ascontiguousarray(b_o.reshape(8, 128).T),
        Wr=np.ascontiguousarray(np.asarray(L0_ffn_Wr, dtype=np.float32)),
        rbt=np.ascontiguousarray(
            np.tile(np.asarray(L0_ffn_rb, dtype=np.float32).reshape(1, 8), (1, 8))),
        W1=np.ascontiguousarray(np.asarray(L0_ffn_W1, dtype=np.float32)),
        W2=np.ascontiguousarray(np.asarray(L0_ffn_W2, dtype=np.float32)),
        W3=np.ascontiguousarray(np.asarray(L0_ffn_W3, dtype=np.float32)),
        Ws1=np.ascontiguousarray(np.asarray(L0_ffn_Ws1, dtype=np.float32)),
        Ws2=np.ascontiguousarray(np.asarray(L0_ffn_Ws2, dtype=np.float32)),
        Ws3=np.ascontiguousarray(np.asarray(L0_ffn_Ws3, dtype=np.float32)),
        identb=np.eye(128, dtype=ml_dtypes.bfloat16),
        identf=np.eye(128, dtype=np.float32),
        sele=np.repeat(np.eye(8), 128, axis=1).astype(ml_dtypes.bfloat16),
        hmask=np.repeat(np.eye(2), 64, axis=0).astype(np.float32),
    )
    nc = build_nc()
    in_maps = []
    for c in range(8):
        b, hf = c // 2, c % 2
        m = dict(arrs)
        m["x_kv"] = np.ascontiguousarray(x[b])
        m["x_own"] = np.ascontiguousarray(x[b, hf * 2048:(hf + 1) * 2048])
        in_maps.append(m)
    res = run_bass_kernel_spmd(nc, in_maps, core_ids=list(range(8)))
    out = np.empty((4, 4096, 1024), dtype=np.float32)
    for c in range(8):
        b, hf = c // 2, c % 2
        out[b, hf * 2048:(hf + 1) * 2048] = res.results[c]["out"]
    return out
